# version 11
# baseline (speedup 1.0000x reference)
import numpy as np
import ml_dtypes
import concourse.bass as bass
import concourse.mybir as mybir
from concourse.bass_utils import run_bass_kernel_spmd

F32 = mybir.dt.float32
BF16 = mybir.dt.bfloat16
AF = mybir.ActivationFunctionType
ALU = mybir.AluOpType
AX = mybir.AxisListType

D = 1024
T = 4096
SEG = 512
NSEG = 8
NH = 8
DH = 128
II = 2816
NI = 22
NC8 = 8
ISD = float(1.0 / np.sqrt(np.float32(128.0)))
EPS = 1e-6
LNEPS = 1e-5
NBW = [512, 512, 512, 512, 512, 256]
NBO = [0, 512, 1024, 1536, 2048, 2560]

DEBUG_Y = False
DEBUG2 = False
SEMS = ["s_vf", "s_ld", "s_x", "s_xb", "s_tr", "s_y", "s_yl", "s_pn", "s_wc", "s_ws", "s_o",
        "s_mm", "s_ev", "s_ve", "s_gp"]


def build_program():
    nc = bass.Bass("TRN2", target_bir_lowering=False)

    x_d = nc.dram_tensor("x", [T, D], F32, kind="ExternalInput").ap()
    wq_d = nc.dram_tensor("wq", [D, D], F32, kind="ExternalInput").ap()
    wk_d = nc.dram_tensor("wk", [D, D], F32, kind="ExternalInput").ap()
    wv_d = nc.dram_tensor("wv", [D, D], F32, kind="ExternalInput").ap()
    bq_d = nc.dram_tensor("bq8", [128, 8], F32, kind="ExternalInput").ap()
    bk_d = nc.dram_tensor("bk8", [128, 8], F32, kind="ExternalInput").ap()
    bv_d = nc.dram_tensor("bvrow", [1, D], F32, kind="ExternalInput").ap()
    beta_d = nc.dram_tensor("beta", [1, NH], F32, kind="ExternalInput").ap()
    wg_d = nc.dram_tensor("wg", [II, D], F32, kind="ExternalInput").ap()
    wu_d = nc.dram_tensor("wu", [II, D], F32, kind="ExternalInput").ap()
    wd_d = nc.dram_tensor("wd", [D, II], F32, kind="ExternalInput").ap()
    tril_d = nc.dram_tensor("tril", [128, 128], BF16, kind="ExternalInput").ap()
    out_d = nc.dram_tensor("out", [T, D], F32, kind="ExternalOutput").ap()
    ydbg_d = nc.dram_tensor("ydbg", [T, D], F32, kind="ExternalOutput").ap() if DEBUG_Y else None
    if DEBUG2:
        dbg_xT = nc.dram_tensor("dbg_xT", [128, 4096], BF16, kind="ExternalOutput").ap()
        dbg_qT = nc.dram_tensor("dbg_qT", [128, 4096], BF16, kind="ExternalOutput").ap()
        dbg_kT = nc.dram_tensor("dbg_kT", [128, 4096], BF16, kind="ExternalOutput").ap()
        dbg_sq = nc.dram_tensor("dbg_sq", [128, 4096], BF16, kind="ExternalOutput").ap()
        dbg_v = nc.dram_tensor("dbg_v", [128, 4096], BF16, kind="ExternalOutput").ap()
        dbg_at = nc.dram_tensor("dbg_at", [128, 4096], F32, kind="ExternalOutput").ap()
        dbg_P = nc.dram_tensor("dbg_P", [128, 1280], BF16, kind="ExternalOutput").ap()
        dbg_st = nc.dram_tensor("dbg_st", [128, 16], F32, kind="ExternalOutput").ap()
        dbg_st2 = nc.dram_tensor("dbg_st2", [128, 32], F32, kind="ExternalOutput").ap()
        dbg_am = nc.dram_tensor("dbg_am", [128, 128], F32, kind="ExternalOutput").ap()
        dbg_sc = nc.dram_tensor("dbg_sc", [128, 512], F32, kind="ExternalOutput").ap()

    y_d = nc.dram_tensor("yscratch", [T, D], F32, kind="Internal").ap()
    wgq_d = nc.dram_tensor("wgq", [128, NI * 8 * 128], BF16, kind="Internal").ap()
    wuq_d = nc.dram_tensor("wuq", [128, NI * 8 * 128], BF16, kind="Internal").ap()
    wdq_d = nc.dram_tensor("wdq", [128, 8 * NI * 128], BF16, kind="Internal").ap()

    SA = lambda name, shape, dt=F32, **kw: nc.alloc_sbuf_tensor(name, shape, dt, **kw).ap()
    tril = SA("trilsb", [128, 128], BF16)
    onesrow = SA("onesrow", [1, 128], BF16)
    bq8 = SA("bq8s", [128, 8])
    bk8 = SA("bk8s", [128, 8])
    bvrow = SA("bvrows", [1, D], BF16)
    beta_r = SA("betar", [1, NH])
    grow = SA("grow", [1, NH])
    growbf = SA("growbf", [1, NH], BF16)
    g8 = SA("g8", [128, NH])
    gm8 = SA("gm8", [128, NH])
    stf = SA("stf", [128, 64])
    rk_tmp = SA("rktmp", [128, 128])
    thr = SA("thr", [128, 8])
    mnb = SA("mnb", [128, 512], BF16)
    exb = SA("exb", [128, 512], BF16)

    base = nc._sbuf_addr_for_side("left")
    base = (base + 511) & ~511
    cur = [base]

    def AR(name, shape, dt):
        nbytes = int(np.prod(shape[1:])) * (2 if dt == BF16 else 4)
        off = cur[0]
        h = nc.alloc_sbuf_tensor_at(name, shape, dt, offset=off)
        cur[0] = (off + nbytes + 63) & ~63
        return h.ap()

    a0 = cur[0]
    wq_sb = AR("wq_sb", [128, 8 * D], BF16)
    wk_sb = AR("wk_sb", [128, 8 * D], BF16)
    wv_sb = AR("wv_sb", [128, 8 * D], BF16)
    Mf = AR("Mhatf", [128, NH * 132], F32)
    Mb = AR("Mhatb", [128, NH * 132], BF16)
    xf = AR("xf", [128, 2 * D], F32)
    xbf = AR("xbf", [128, 4 * D], BF16)
    xT = AR("xT", [128, 4 * D], BF16)
    qT = AR("qT", [128, NH * 512], BF16)
    kT = AR("kT", [128, NH * 512], BF16)
    sqT = AR("sqT", [128, NH * 512], BF16)
    skT = AR("skT", [128, NH * 512], BF16)
    sk_tok = AR("sk_tok", [128, NH * 512], BF16)
    vbf = AR("vbf", [128, 4 * D], BF16)
    P_sb = AR("P_sb", [128, 1280], BF16)
    PT_sb = AR("PT_sb", [128, 1280], BF16)
    u_sb = AR("u_sb", [128, 4 * 136], BF16)
    attn = AR("attn", [128, 4 * D], F32)
    ybuf = AR("ybuf", [128, D], F32)
    a_end = cur[0]

    cur[0] = a0
    y4 = AR("y4", [128, 4 * D], F32)
    xq = AR("xq", [128, 4 * D], BF16)
    xqT = AR("xqT", [128, 4 * D], BF16)
    pnl_g = AR("pnl_g", [128, 4 * D], BF16)
    pnl_u = AR("pnl_u", [128, 4 * D], BF16)
    wdq_sb = AR("wdq_sb", [128, 8 * NI * 128], BF16)
    hq_full = AR("hq_full", [128, 4 * II], BF16)
    hqT = AR("hqT", [128, 2 * II], BF16)
    h_sl = AR("h_sl", [128, 2 * 512], F32)
    silb = AR("silb", [128, 512], F32)
    r2b = AR("r2b", [128, D], F32)
    obuf = AR("obuf", [128, D], F32)
    b_end = cur[0]

    cur[0] = (b_end + 63) & ~63
    qt_u2 = AR("qt_u2", [128, 512], F32)
    s1b2 = AR("s1b2", [128, 512], BF16)
    s3b2 = AR("s3b2", [128, 512], BF16)
    s5b2 = AR("s5b2", [128, 512], BF16)
    s10b2 = AR("s10b2", [128, 512], BF16)
    s14b2 = AR("s14b2", [128, 512], BF16)
    sgb2 = AR("sgb2", [128, 512], BF16)
    wSh2 = AR("wSh2", [128, 32], BF16)
    assert cur[0] <= a_end, f"shadow overflow {cur[0]} > {a_end}"

    cur[0] = (max(a_end, b_end) + 511) & ~511
    wpc = AR("wpc", [128, 2 * 1536], F32)
    qslc = AR("qslc", [128, 512], BF16)
    ttmp = AR("ttmp", [128, II], BF16)
    qt_uh = AR("qt_uh", [128, 512], F32)
    qt_u = AR("qt_u", [128, 512], F32)
    s1b = AR("s1b", [128, 512], BF16)
    s3b = AR("s3b", [128, 512], BF16)
    s5b = AR("s5b", [128, 512], BF16)
    s10b = AR("s10b", [128, 512], BF16)
    s14b = AR("s14b", [128, 512], BF16)
    sgb = AR("sgb", [128, 512], BF16)
    c7b = AR("c7b", [128, 512], BF16)
    c20b = AR("c20b", [128, 512], BF16)
    m1b = AR("m1b", [128, 512], BF16)
    wm_ = AR("wm_", [128, 32], F32)
    wS_ = AR("wS_", [128, 32], F32)
    wR_ = AR("wR_", [128, 32], F32)
    wSh = AR("wSh", [128, 32], BF16)
    w_end = cur[0]
    assert w_end <= 212000, f"sbuf overflow: {w_end}"
    import os
    if os.environ.get("PRINT_SBUF"): print("w_end:", w_end, "slack:", 212000 - w_end, "a_end:", a_end, "b_end:", b_end)

    ps = [nc.alloc_psum_tensor(f"ps{i}", [128, 512], F32).ap() for i in range(8)]

    sems = {k: nc.alloc_semaphore(k) for k in SEMS}

    class Wk:
        def __init__(self, name, h):
            self.name = name
            self.T = h if name == "tensor" else None
            self.V = h if name == "vector" else None
            self.A = h if name == "scalar" else None
            self.G = h if name == "gpsimd" else None
            self.S = h if name == "sync" else None
            self.c = {k: 0 for k in SEMS}
            self.bank = {}
            self.bias_free = 0
            self.silb_free = 0

        def __post(self):
            pass

        def inc(self, ins, sem, k=1):
            self.c[sem] += k
            if ins is not None:
                ins.then_inc(sems[sem], k)

        def bank_set(self, b, sem):
            self.bank[b] = (sem, self.c[sem])

        def bank_wait(self, b):
            if b in self.bank:
                sem, v = self.bank[b]
                self.w(self.T, sem, v)

        def w(self, h, sem, val=None):
            if h is not None:
                h.wait_ge(sems[sem], self.c[sem] if val is None else val)

    def vf(wk, ins):
        wk.inc(ins, "s_vf")
        wk.w(wk.V, "s_vf")

    def bc16(small, nb16):
        return small[:, 0:nb16].unsqueeze(2).broadcast_to([128, nb16, 16])


    def quant_slice(wk, src, dst, n):
        nb16 = n // 16
        v3 = lambda ap: ap.rearrange("p (k f) -> p k f", f=16)
        wk.w(wk.V, "s_ev")
        wk.w(wk.V, "s_gp")
        wk.w(wk.V, "s_ve")
        ins = wk.V and wk.V.tensor_reduce(out=wm_[:, 0:nb16], in_=v3(src), axis=AX.X,
                                          op=ALU.max, apply_absolute_value=True)
        vf(wk, wk.V and ins)
        ins = wk.V and wk.V.tensor_scalar(out=wS_[:, 0:nb16], in0=wm_[:, 0:nb16], scalar1=float(1.0 / 6.0),
                                          scalar2=1e-12, op0=ALU.mult, op1=ALU.max)
        wk.inc(wk.V and ins, "s_ve")
        wk.w(wk.A, "s_ve")
        if wk.A:
            wk.A.activation(out=wR_[:, 0:nb16], in_=wS_[:, 0:nb16], func=AF.Ln)
            wk.A.activation(out=wR_[:, 0:nb16], in_=wR_[:, 0:nb16], func=AF.Exp, scale=-1.0)
            wk.A.activation(out=qt_uh[:, 0:n], in_=src, func=AF.Abs)
            ins = wk.A.activation(out=sgb[:, 0:n], in_=src, func=AF.Sign)
        wk.inc(wk.A and ins, "s_ev")
        wk.w(wk.V, "s_ev")
        if wk.V:
            wk.V.tensor_scalar(out=wSh[:, 0:nb16], in0=wS_[:, 0:nb16], scalar1=0.5,
                               scalar2=None, op0=ALU.mult)
            ins = wk.V.scalar_tensor_tensor(out=v3(qt_u[:, 0:n]), in0=v3(qt_uh[:, 0:n]), scalar=1.0,
                                            in1=bc16(wR_, nb16), op0=ALU.mult, op1=ALU.mult)
        wk.inc(wk.V and ins, "s_ve")
        wk.w(wk.A, "s_ve")
        if wk.A:
            wk.A.activation(out=s1b[:, 0:n], in_=qt_u[:, 0:n], func=AF.Sign, bias=thr[:, 0:1])
            wk.A.activation(out=s3b[:, 0:n], in_=qt_u[:, 0:n], func=AF.Sign, bias=thr[:, 1:2])
            wk.A.activation(out=s5b[:, 0:n], in_=qt_u[:, 0:n], func=AF.Sign, bias=thr[:, 2:3])
            wk.A.activation(out=s10b[:, 0:n], in_=qt_u[:, 0:n], func=AF.Sign, bias=thr[:, 3:4])
            ins = wk.A.activation(out=s14b[:, 0:n], in_=qt_u[:, 0:n], func=AF.Sign, bias=thr[:, 4:5])
        wk.inc(wk.A and ins, "s_ev")
        wk.w(wk.V, "s_ev")
        if wk.V:
            wk.V.tensor_scalar(out=c7b[:, 0:n], in0=qt_u[:, 0:n], scalar1=1.75, scalar2=None, op0=ALU.is_gt)
            wk.V.tensor_scalar(out=c20b[:, 0:n], in0=qt_u[:, 0:n], scalar1=5.0, scalar2=2.0,
                               op0=ALU.is_gt, op1=ALU.mult)
            wk.V.tensor_tensor(out=s1b[:, 0:n], in0=s1b[:, 0:n], in1=s3b[:, 0:n], op=ALU.add)
            wk.V.tensor_tensor(out=s1b[:, 0:n], in0=s1b[:, 0:n], in1=s5b[:, 0:n], op=ALU.add)
            wk.V.tensor_tensor(out=s10b[:, 0:n], in0=s10b[:, 0:n], in1=s14b[:, 0:n], op=ALU.add)
            wk.V.scalar_tensor_tensor(out=s1b[:, 0:n], in0=s1b[:, 0:n], scalar=0.5, in1=c7b[:, 0:n],
                                      op0=ALU.mult, op1=ALU.add)
            wk.V.tensor_tensor(out=s1b[:, 0:n], in0=s1b[:, 0:n], in1=s10b[:, 0:n], op=ALU.add)
            wk.V.scalar_tensor_tensor(out=s1b[:, 0:n], in0=c20b[:, 0:n], scalar=2.0, in1=s1b[:, 0:n],
                                      op0=ALU.mult, op1=ALU.add)
            ins = wk.V.scalar_tensor_tensor(out=m1b[:, 0:n], in0=s1b[:, 0:n], scalar=3.5, in1=sgb[:, 0:n],
                                            op0=ALU.add, op1=ALU.mult)
        else:
            ins = None
        wk.inc(wk.V and ins, "s_ve")
        wk.w(wk.G, "s_ve")
        ins = wk.G and wk.G.tensor_tensor(out=v3(dst), in0=v3(m1b[:, 0:n]),
                                          in1=bc16(wSh, nb16), op=ALU.mult)
        wk.inc(ins, "s_gp")

    QT = [dict(qt_u=qt_u, s1b=s1b, s3b=s3b, s5b=s5b, s10b=s10b, s14b=s14b, sgb=sgb, wSh=wSh),
          dict(qt_u=qt_u2, s1b=s1b2, s3b=s3b2, s5b=s5b2, s10b=s10b2, s14b=s14b2, sgb=sgb2, wSh=wSh2)]

    class QPipe:

        def __init__(self, wk):
            self.wk = wk
            self.k = 0
            self.pending = None
            self.s5_done = [0, 0]
            self.s6_done = [0, 0]
            self.s4_done = [0, 0]
            self.s2_done = 0
            self.s3_done = 0

        def push(self, src, dst, n):
            wk = self.wk
            par = self.k % 2
            t = QT[par]
            nb16 = n // 16
            v3 = lambda ap: ap[:, 0:n].rearrange("p (kk f) -> p kk f", f=16)
            wk.w(wk.V, "s_ev", self.s2_done)
            wk.w(wk.V, "s_ve")
            ins = wk.V and wk.V.tensor_reduce(out=wm_[:, 0:nb16], in_=src.rearrange("p (kk f) -> p kk f", f=16),
                                              axis=AX.X, op=ALU.max, apply_absolute_value=True)
            vf(wk, wk.V and ins)
            ins = wk.V and wk.V.tensor_scalar(out=wS_[:, 0:nb16], in0=wm_[:, 0:nb16], scalar1=float(1.0 / 6.0),
                                              scalar2=1e-12, op0=ALU.mult, op1=ALU.max)
            wk.inc(wk.V and ins, "s_ve")
            wk.w(wk.A, "s_ve", self.s3_done)
            wk.w(wk.A, "s_ve", self.s5_done[par])
            wk.w(wk.A, "s_ve")
            if wk.A:
                wk.A.activation(out=wR_[:, 0:nb16], in_=wS_[:, 0:nb16], func=AF.Ln)
                wk.A.activation(out=wR_[:, 0:nb16], in_=wR_[:, 0:nb16], func=AF.Exp, scale=-1.0)
                wk.A.activation(out=qt_uh[:, 0:n], in_=src, func=AF.Abs)
                ins = wk.A.activation(out=t["sgb"][:, 0:n], in_=src, func=AF.Sign)
            wk.inc(wk.A and ins, "s_ev")
            self.s2_done = wk.c["s_ev"]
            wk.w(wk.V, "s_ev", self.s4_done[par])
            wk.w(wk.V, "s_gp", self.s6_done[par])
            wk.w(wk.V, "s_ev", self.s2_done)
            if wk.V:
                wk.V.tensor_scalar(out=t["wSh"][:, 0:nb16], in0=wS_[:, 0:nb16], scalar1=0.5,
                                   scalar2=None, op0=ALU.mult)
                ins = wk.V.scalar_tensor_tensor(out=v3(t["qt_u"]), in0=v3(qt_uh), scalar=1.0,
                                                in1=bc16(wR_, nb16), op0=ALU.mult, op1=ALU.mult)
            wk.inc(wk.V and ins, "s_ve")
            self.s3_done = wk.c["s_ve"]
            wk.w(wk.A, "s_ve", self.s3_done)
            if wk.A:
                wk.A.activation(out=t["s1b"][:, 0:n], in_=t["qt_u"][:, 0:n], func=AF.Sign, bias=thr[:, 0:1])
                wk.A.activation(out=t["s3b"][:, 0:n], in_=t["qt_u"][:, 0:n], func=AF.Sign, bias=thr[:, 1:2])
                wk.A.activation(out=t["s5b"][:, 0:n], in_=t["qt_u"][:, 0:n], func=AF.Sign, bias=thr[:, 2:3])
                wk.A.activation(out=t["s10b"][:, 0:n], in_=t["qt_u"][:, 0:n], func=AF.Sign, bias=thr[:, 3:4])
                ins = wk.A.activation(out=t["s14b"][:, 0:n], in_=t["qt_u"][:, 0:n], func=AF.Sign, bias=thr[:, 4:5])
            wk.inc(wk.A and ins, "s_ev")
            self.s4_done[par] = wk.c["s_ev"]
            prev = self.pending
            self.pending = (dst, n, par, self.s4_done[par])
            self.k += 1
            if prev is not None:
                self._emit_tail(prev)

        def flush(self):
            if self.pending is not None:
                self._emit_tail(self.pending)
                self.pending = None

        def _emit_tail(self, job):
            wk = self.wk
            dst, n, par, s4cnt = job
            t = QT[par]
            nb16 = n // 16
            v3 = lambda ap: ap[:, 0:n].rearrange("p (kk f) -> p kk f", f=16)
            wk.w(wk.V, "s_ev", s4cnt)
            wk.w(wk.V, "s_gp", self.s6_done[1 - par])
            if wk.V:
                wk.V.tensor_scalar(out=c7b[:, 0:n], in0=t["qt_u"][:, 0:n], scalar1=1.75, scalar2=None, op0=ALU.is_gt)
                wk.V.tensor_scalar(out=c20b[:, 0:n], in0=t["qt_u"][:, 0:n], scalar1=5.0, scalar2=2.0,
                                   op0=ALU.is_gt, op1=ALU.mult)
                wk.V.tensor_tensor(out=t["s1b"][:, 0:n], in0=t["s1b"][:, 0:n], in1=t["s3b"][:, 0:n], op=ALU.add)
                wk.V.tensor_tensor(out=t["s1b"][:, 0:n], in0=t["s1b"][:, 0:n], in1=t["s5b"][:, 0:n], op=ALU.add)
                wk.V.tensor_tensor(out=t["s10b"][:, 0:n], in0=t["s10b"][:, 0:n], in1=t["s14b"][:, 0:n], op=ALU.add)
                wk.V.scalar_tensor_tensor(out=t["s1b"][:, 0:n], in0=t["s1b"][:, 0:n], scalar=0.5,
                                          in1=c7b[:, 0:n], op0=ALU.mult, op1=ALU.add)
                wk.V.tensor_tensor(out=t["s1b"][:, 0:n], in0=t["s1b"][:, 0:n], in1=t["s10b"][:, 0:n], op=ALU.add)
                wk.V.scalar_tensor_tensor(out=t["s1b"][:, 0:n], in0=c20b[:, 0:n], scalar=2.0,
                                          in1=t["s1b"][:, 0:n], op0=ALU.mult, op1=ALU.add)
                ins = wk.V.scalar_tensor_tensor(out=m1b[:, 0:n], in0=t["s1b"][:, 0:n], scalar=3.5,
                                                in1=t["sgb"][:, 0:n], op0=ALU.add, op1=ALU.mult)
            else:
                ins = None
            wk.inc(wk.V and ins, "s_ve")
            self.s5_done[par] = wk.c["s_ve"]
            wk.w(wk.G, "s_ve", self.s5_done[par])
            ins = wk.G and wk.G.tensor_tensor(out=v3(dst), in0=v3(m1b), in1=bc16(t["wSh"], nb16), op=ALU.mult)
            wk.inc(ins, "s_gp")
            self.s6_done[par] = wk.c["s_gp"]

    WCH = ([("g", r, 0, D, True, D) for r in range(NI)]
           + [("u", r, 0, D, True, D) for r in range(NI)])
    for rn in range(8):
        WCH.append(("d", rn, 0, 1536, False, II))
        WCH.append(("d", rn, 1536, 1280, True, II))

    def w_emit_load(wk, idx):
        kind, r, off, wid, last, F = WCH[idx]
        src = {"g": wg_d, "u": wu_d, "d": wd_d}[kind]
        buf = wpc[:, (idx % 2) * 1536: (idx % 2) * 1536 + wid]
        ins = wk.A and wk.A.dma_start(out=buf, in_=src[r * 128:(r + 1) * 128, off:off + wid])
        wk.inc(ins, "s_wc", 16)

    def w_emit_chunk(wk, idx):
        kind, r, off, wid, last, F = WCH[idx]
        buf = wpc[:, (idx % 2) * 1536:]
        wk.w(wk.V, "s_wc")
        pos = 0
        while pos < wid:
            n = min(512, wid - pos)
            quant_slice(wk, buf[:, pos:pos + n], qslc[:, 0:n], n)
            j0 = (off + pos) // 128
            nt = n // 128
            wk.w(wk.S, "s_gp")
            wk.w(wk.S, "s_ws")
            ins = wk.S and wk.S.dma_start(
                out=ttmp[:, j0 * 128: (j0 + nt) * 128].rearrange("p (k f) -> p k f", f=128),
                in_=qslc[:, 0:n].rearrange("p (k f) -> p k f", f=128), transpose=True)
            wk.inc(ins, "s_tr", 16)
            pos += n
        if idx + 1 < len(WCH):
            wk.w(wk.A, "s_ve")
            w_emit_load(wk, idx + 1)
        if last:
            wk.w(wk.A, "s_tr")
            dstt = {"g": wgq_d, "u": wuq_d, "d": wdq_d}[kind]
            ins = wk.A and wk.A.dma_start(out=dstt[:, r * F: (r + 1) * F], in_=ttmp[:, 0:F])
            wk.inc(ins, "s_ws", 16)

    def barrier(wk):
        for h in (wk.T, wk.V, wk.A, wk.G, wk.S):
            if h is not None:
                for s in ("s_mm", "s_ev", "s_ve", "s_gp"):
                    wk.w(h, s)
        wk.w(wk.V, "s_ve")

    def dbg_dump(wk, dst, src):
        for sem in ("s_mm", "s_ev", "s_ve", "s_gp"):
            wk.w(wk.G, sem)
        ins = wk.G and wk.G.dma_start(out=dst, in_=src)
        wk.inc(ins, "s_o", 16)
        for h in (wk.T, wk.V, wk.A, wk.S):
            wk.w(h, "s_o")

    def ln_apply(wk, src, dst, trash):
        wk.w(wk.V, "s_ve")
        ins = wk.V and wk.V.tensor_reduce(out=stf[:, 40:41], in_=src, axis=AX.X, op=ALU.add)
        wk.inc(wk.V and ins, "s_ve")
        wk.w(wk.A, "s_ve")
        ins = wk.A and wk.A.activation(out=trash, in_=src, func=AF.Square, accum_out=stf[:, 42:43])
        wk.inc(ins, "s_ev")
        wk.w(wk.V, "s_ev")
        if wk.V:
            wk.V.tensor_scalar(out=stf[:, 44:45], in0=stf[:, 40:41], scalar1=float(1.0 / D),
                               scalar2=None, op0=ALU.mult)
            ins = wk.V.tensor_scalar(out=stf[:, 46:47], in0=stf[:, 42:43], scalar1=float(1.0 / D),
                                     scalar2=None, op0=ALU.mult)
        else:
            ins = None
        vf(wk, wk.V and ins)
        ins = wk.V and wk.V.tensor_tensor(out=stf[:, 48:49], in0=stf[:, 44:45], in1=stf[:, 44:45], op=ALU.mult)
        vf(wk, wk.V and ins)
        ins = wk.V and wk.V.tensor_tensor(out=stf[:, 50:51], in0=stf[:, 46:47], in1=stf[:, 48:49], op=ALU.subtract)
        vf(wk, wk.V and ins)
        ins = wk.V and wk.V.tensor_scalar(out=stf[:, 52:53], in0=stf[:, 50:51], scalar1=LNEPS,
                                          scalar2=None, op0=ALU.add)
        wk.inc(wk.V and ins, "s_ve")
        wk.w(wk.A, "s_ve")
        if wk.A:
            wk.A.activation(out=stf[:, 54:55], in_=stf[:, 52:53], func=AF.Ln)
            ins = wk.A.activation(out=stf[:, 56:57], in_=stf[:, 54:55], func=AF.Exp, scale=-0.5)
        wk.inc(wk.A and ins, "s_ev")
        wk.w(wk.V, "s_ev")
        ins = wk.V and wk.V.tensor_tensor(out=stf[:, 58:59], in0=stf[:, 44:45], in1=stf[:, 56:57], op=ALU.mult)
        vf(wk, wk.V and ins)
        ins = wk.V and wk.V.tensor_scalar(out=stf[:, 60:61], in0=stf[:, 58:59], scalar1=-1.0,
                                          scalar2=None, op0=ALU.mult)
        wk.inc(wk.V and ins, "s_ve")
        wk.w(wk.A, "s_ve")
        ins = wk.A and wk.A.activation(out=dst, in_=src, func=AF.Identity, scale=stf[:, 56:57], bias=stf[:, 60:61])
        wk.inc(ins, "s_ev")

    def emit_startup(wk):
        if wk.A:
            wk.A.add_instruction(mybir.InstLoadActFuncSet(
                name=nc.get_next_instruction_name(), act_func_set_id=6, ins=[], outs=[]))
        for dst, src in ((tril, tril_d), (bq8, bq_d), (bk8, bk_d), (beta_r, beta_d)):
            ins = wk.A and wk.A.dma_start(out=dst[:, :], in_=src[:, :])
            wk.inc(ins, "s_ld", 16)
        for tc in range(2):
            ins = wk.A and wk.A.dma_start(out=xf[:, tc * D:(tc + 1) * D],
                                          in_=x_d[tc * 128:(tc + 1) * 128, :])
            wk.inc(ins, "s_x", 16)
        for dst, src in ((wq_sb, wq_d), (wk_sb, wk_d), (wv_sb, wv_d)):
            ins = wk.G and wk.G.dma_start(out=dst.rearrange("p (j n) -> p j n", n=D),
                                          in_=src.rearrange("(j p) n -> p j n", p=128))
            wk.inc(ins, "s_ld", 16)
        ins = wk.G and wk.G.dma_start(out=bvrow[:, :], in_=bv_d[:, :])
        wk.inc(ins, "s_ld", 16)
        ins = wk.G and wk.G.dma_start(out=xbf.rearrange("p (tc c) -> p tc c", c=D),
                                      in_=x_d[0:SEG, :].rearrange("(tc p) c -> p tc c", p=128))
        wk.inc(ins, "s_xb", 16)
        if wk.V:
            for col, val in ((0, -0.25), (1, -0.75), (2, -1.25), (3, -2.5), (4, -3.5)):
                wk.V.memset(thr[:, col:col + 1], val)
            wk.V.memset(Mf[:, :], 0.0)
            wk.V.memset(Mb[:, :], 0.0)
            wk.V.memset(onesrow[:, :], 1.0)
            for tc in range(4):
                ins = wk.V.memset(u_sb[:, tc * 136 + 128: tc * 136 + 129], 1.0)
        wk.inc(wk.V and ins, "s_ve")
        wk.w(wk.A, "s_ld", 16 * 4)
        if wk.A:
            wk.A.activation(out=grow[:, :], in_=beta_r[:, :], func=AF.Exp, scale=-1.0)
            wk.A.activation(out=grow[:, :], in_=grow[:, :], func=AF.Ln, bias=1.0)
            wk.A.activation(out=grow[:, :], in_=grow[:, :], func=AF.Exp, scale=-1.0)
            ins = wk.A.copy(out=growbf[:, :], in_=grow[:, :])
        wk.inc(wk.A and ins, "s_ev")
        wk.w(wk.T, "s_ev")
        wk.w(wk.T, "s_ve")
        ins = wk.T and wk.T.matmul(ps[7][:, 0:NH], onesrow[0:1, :], growbf[0:1, :], start=True, stop=True)
        wk.inc(ins, "s_mm")
        wk.w(wk.V, "s_mm")
        ins = wk.V and wk.V.tensor_copy(out=g8[:, :], in_=ps[7][:, 0:NH])
        vf(wk, wk.V and ins)
        ins = wk.V and wk.V.tensor_scalar(out=gm8[:, :], in0=g8[:, :], scalar1=-1.0, scalar2=1.0,
                                          op0=ALU.mult, op1=ALU.add)
        wk.inc(wk.V and ins, "s_ve")
        w_emit_load(wk, 0)

    def proj_qk(wk, which):
        w_sb = wq_sb if which == "q" else wk_sb
        b8 = bq8 if which == "q" else bk8
        oT = qT if which == "q" else kT
        oS = sqT if which == "q" else skT
        xTr = xT.rearrange("p (tc j2 f) -> p j2 tc f", j2=8, f=128)
        for h in range(NH):
            bi = h % 2
            bank = ps[bi]
            wk.bank_wait(bi)
            if wk.T:
                for j in range(NC8):
                    ins = wk.T.matmul(bank[:, 0:512],
                                      w_sb[:, j * D + h * 128: j * D + h * 128 + 128],
                                      xTr[:, j, :, :],
                                      start=(j == 0), stop=(j == NC8 - 1))
            wk.inc(wk.T and ins, "s_mm")
            wk.w(wk.A, "s_mm")
            ins = wk.A and wk.A.activation(out=oT[:, h * 512:(h + 1) * 512], in_=bank[:, 0:512],
                                           func=AF.Identity, bias=b8[:, h:h + 1])
            wk.inc(ins, "s_ev")
            wk.bank_set(bi, "s_ev")
            wk.w(wk.V, "s_ev")
            ins = wk.V and wk.V.tensor_scalar(out=mnb[:, :], in0=oT[:, h * 512:(h + 1) * 512],
                                              scalar1=0.0, scalar2=None, op0=ALU.min)
            wk.inc(wk.V and ins, "s_ve")
            wk.w(wk.A, "s_ve")
            ins = wk.A and wk.A.activation(out=exb[:, :], in_=mnb[:, :], func=AF.Exp)
            wk.inc(wk.A and ins, "s_ev")
            wk.w(wk.V, "s_ev")
            if wk.V:
                wk.V.tensor_scalar(out=oS[:, h * 512:(h + 1) * 512], in0=oT[:, h * 512:(h + 1) * 512],
                                   scalar1=0.0, scalar2=None, op0=ALU.max)
                ins = wk.V.tensor_tensor(out=oS[:, h * 512:(h + 1) * 512],
                                         in0=oS[:, h * 512:(h + 1) * 512], in1=exb[:, :], op=ALU.add)
            wk.inc(wk.V and ins, "s_ve")

    OFFP = [0, 128, 384, 768]

    def emit_segment(wk, s):
        barrier(wk)
        wk.w(wk.S, "s_xb")
        ins = wk.S and wk.S.dma_start(out=xT.rearrange("p (k f) -> p k f", f=128),
                                      in_=xbf.rearrange("p (k f) -> p k f", f=128), transpose=True)
        wk.inc(ins, "s_tr", 16)
        tr_xT = wk.c["s_tr"]
        if s + 1 < NSEG:
            wk.w(wk.G, "s_tr", tr_xT)
            ins = wk.G and wk.G.dma_start(
                out=xbf.rearrange("p (tc c) -> p tc c", c=D),
                in_=x_d[(s + 1) * SEG:(s + 2) * SEG, :].rearrange("(tc p) c -> p tc c", p=128))
            wk.inc(ins, "s_xb", 16)
        wk.w(wk.T, "s_tr", tr_xT)
        wk.w(wk.T, "s_ld", 16 * 8)
        proj_qk(wk, "q")
        proj_qk(wk, "k")
        for tc in range(4):
            for half in range(2):
                bi = 2 + (tc * 2 + half) % 2
                bank = ps[bi]
                wk.bank_wait(bi)
                if wk.T:
                    for j in range(NC8):
                        wk.T.matmul(bank[:, 0:512],
                                    xT[:, (tc * 8 + j) * 128:(tc * 8 + j) * 128 + 128],
                                    wv_sb[:, j * D + half * 512: j * D + half * 512 + 512],
                                    start=(j == 0), stop=False)
                    ins = wk.T.matmul(bank[:, 0:512], onesrow[0:1, :],
                                      bvrow[0:1, half * 512:(half + 1) * 512], start=False, stop=True)
                wk.inc(wk.T and ins, "s_mm")
                wk.w(wk.A, "s_mm")
                ins = wk.A and wk.A.activation(out=vbf[:, tc * D + half * 512: tc * D + (half + 1) * 512],
                                               in_=bank[:, 0:512], func=AF.Copy)
                wk.inc(ins, "s_ev")
                wk.bank_set(bi, "s_ev")
        wk.w(wk.S, "s_ve")
        ins = wk.S and wk.S.dma_start(out=sk_tok.rearrange("p (k f) -> p k f", f=128),
                                      in_=skT.rearrange("p (k f) -> p k f", f=128), transpose=True)
        wk.inc(ins, "s_tr", 16)
        if DEBUG2 and s == 0:
            for dst, src in ((dbg_xT, xT), (dbg_qT, qT), (dbg_kT, kT), (dbg_sq, sqT), (dbg_v, vbf)):
                dbg_dump(wk, dst[:, :], src[:, :])
        adot_mm_prev = wk.c["s_mm"]
        for h in range(NH):
            wk.w(wk.A, "s_tr")
            for i4 in range(4):
                bank = ps[4 + i4 % 2]
                wk.w(wk.T, "s_ev")
                N = (i4 + 1) * 128
                ins = wk.T and wk.T.matmul(bank[:, 0:N],
                                           qT[:, h * 512 + i4 * 128: h * 512 + (i4 + 1) * 128],
                                           kT[:, h * 512: h * 512 + N], start=True, stop=True)
                wk.inc(wk.T and ins, "s_mm")
                wk.w(wk.V, "s_mm")
                wk.w(wk.V, "s_ev")
                ins = wk.V and wk.V.tensor_reduce(out=stf[:, 0:1], in_=bank[:, 0:N], axis=AX.X,
                                                  op=ALU.max, negate=True)
                vf(wk, wk.V and ins)
                ins = wk.V and wk.V.tensor_scalar(out=stf[:, 2:3], in0=stf[:, 0:1], scalar1=ISD,
                                                  scalar2=None, op0=ALU.mult)
                wk.inc(wk.V and ins, "s_ve")
                if DEBUG2 and s == 0 and h == 0:
                    if wk.V:
                        wk.V.tensor_copy(out=stf[:, 26:27], in_=stf[:, 0:1])
                    pass
                wk.w(wk.A, "s_ve")
                if wk.A:
                    if i4 > 0:
                        wk.A.activation(out=P_sb[:, OFFP[i4]: OFFP[i4] + i4 * 128],
                                        in_=bank[:, 0:i4 * 128], func=AF.Exp, scale=ISD,
                                        bias=stf[:, 2:3], accum_out=stf[:, 4:5])
                    ins = wk.A.activation(out=P_sb[:, OFFP[i4] + i4 * 128: OFFP[i4] + N],
                                          in_=bank[:, i4 * 128:N], func=AF.Exp, scale=ISD,
                                          bias=stf[:, 2:3])
                wk.inc(wk.A and ins, "s_ev")
                wk.w(wk.V, "s_ev")
                dg = P_sb[:, OFFP[i4] + i4 * 128: OFFP[i4] + N]
                if wk.V:
                    wk.V.tensor_tensor(out=dg, in0=dg, in1=tril[:, :], op=ALU.mult)
                ins = wk.V and wk.V.tensor_reduce(out=stf[:, 6:7], in_=dg, axis=AX.X, op=ALU.add)
                vf(wk, wk.V and ins)
                if i4 > 0:
                    ins = wk.V and wk.V.tensor_tensor(out=stf[:, 8:9], in0=stf[:, 4:5], in1=stf[:, 6:7], op=ALU.add)
                    vf(wk, wk.V and ins)
                    rs_col = stf[:, 8:9]
                else:
                    rs_col = stf[:, 6:7]
                ins = wk.V and wk.V.reciprocal(out=stf[:, 10:11], in_=rs_col)
                vf(wk, wk.V and ins)
                ins = wk.V and wk.V.tensor_tensor(out=stf[:, 32 + 2 * i4:33 + 2 * i4], in0=stf[:, 10:11],
                                                  in1=gm8[:, h:h + 1], op=ALU.mult)
                vf(wk, wk.V and ins)
                wk.c["s_ve"] += 1
                if wk.V is not None:
                    sems_ins = wk.V.engine_nop()
                    sems_ins.then_inc(sems["s_ve"], 1)
            wk.w(wk.S, "s_ve")
            wk.w(wk.S, "s_mm", adot_mm_prev)
            ins = wk.S and wk.S.dma_start(out=PT_sb.rearrange("p (k f) -> p k f", f=128),
                                          in_=P_sb.rearrange("p (k f) -> p k f", f=128), transpose=True)
            wk.inc(ins, "s_tr", 16)
            tr_P = wk.c["s_tr"]
            for i4 in range(4):
                wk.bank_wait(6)
                if wk.T:
                    wk.T.matmul(ps[6][:, 0:129],
                                sqT[:, h * 512 + i4 * 128: h * 512 + (i4 + 1) * 128],
                                Mb[:, h * 132: h * 132 + 129], start=True, stop=True)
                    ins = wk.T.matmul(ps[6][:, 136:265],
                                      skT[:, h * 512 + i4 * 128: h * 512 + (i4 + 1) * 128],
                                      Mb[:, h * 132: h * 132 + 129], start=True, stop=True)
                wk.inc(wk.T and ins, "s_mm")
                wk.w(wk.V, "s_mm")
                if wk.V:
                    wk.V.tensor_single_scalar(out=stf[:, 16:17], in_=ps[6][:, 128:129], scalar=EPS, op=ALU.add)
                    ins = wk.V.tensor_single_scalar(out=stf[:, 22:23], in_=ps[6][:, 264:265], scalar=EPS, op=ALU.add)
                else:
                    ins = None
                vf(wk, wk.V and ins)
                if wk.V:
                    wk.V.reciprocal(out=stf[:, 18:19], in_=stf[:, 16:17])
                    ins = wk.V.reciprocal(out=stf[:, 24:25], in_=stf[:, 22:23])
                vf(wk, wk.V and ins)
                ins = wk.V and wk.V.tensor_tensor(out=stf[:, 20:21], in0=stf[:, 18:19], in1=g8[:, h:h + 1],
                                                  op=ALU.mult)
                vf(wk, wk.V and ins)
                if wk.V:
                    wk.V.tensor_scalar(out=attn[:, i4 * D + h * 128: i4 * D + (h + 1) * 128],
                                       in0=ps[6][:, 0:128], scalar1=stf[:, 20:21], scalar2=None, op0=ALU.mult)
                    ins = wk.V.tensor_scalar(out=rk_tmp[:, :], in0=ps[6][:, 136:264], scalar1=stf[:, 24:25],
                                             scalar2=None, op0=ALU.mult)
                vf(wk, wk.V and ins)
                ins = wk.V and wk.V.tensor_tensor(out=u_sb[:, i4 * 136: i4 * 136 + 128],
                                                  in0=vbf[:, i4 * D + h * 128: i4 * D + (h + 1) * 128],
                                                  in1=rk_tmp[:, :], op=ALU.subtract)
                wk.inc(wk.V and ins, "s_ve")
                wk.bank_set(6, "s_ve")
            if DEBUG2 and s == 0 and h == 0:
                ins = wk.V and wk.V.tensor_scalar(out=stf[:, 30:31], in0=gm8[:, 0:1], scalar1=1.0,
                                                  scalar2=None, op0=ALU.mult)
                wk.inc(wk.V and ins, "s_ve")
                dbg_dump(wk, dbg_st2[:, :], stf[:, 0:32])
                dbg_dump(wk, dbg_am[:, :], attn[:, 0:128])
            wk.bank_wait(6)
            wk.w(wk.T, "s_ve")
            if wk.T:
                for tc in range(4):
                    ins = wk.T.matmul(ps[6][:, 0:129],
                                      sk_tok[:, (h * 4 + tc) * 128:(h * 4 + tc) * 128 + 128],
                                      u_sb[:, tc * 136: tc * 136 + 129],
                                      start=(tc == 0), stop=(tc == 3))
            wk.inc(wk.T and ins, "s_mm")
            wk.w(wk.V, "s_mm")
            ins = wk.V and wk.V.tensor_tensor(out=Mf[:, h * 132: h * 132 + 129],
                                              in0=Mf[:, h * 132: h * 132 + 129],
                                              in1=ps[6][:, 0:129], op=ALU.add)
            vf(wk, wk.V and ins)
            wk.bank_set(6, "s_vf")
            ins = wk.V and wk.V.tensor_copy(out=Mb[:, h * 132: h * 132 + 129], in_=Mf[:, h * 132: h * 132 + 129])
            wk.inc(wk.V and ins, "s_ve")
            for i4 in range(4):
                wk.w(wk.T, "s_tr", tr_P)
                wk.bank_wait(7)
                if wk.T:
                    for j in range(i4 + 1):
                        ins = wk.T.matmul(ps[7][:, 0:128],
                                          PT_sb[:, OFFP[i4] + j * 128: OFFP[i4] + (j + 1) * 128],
                                          vbf[:, j * D + h * 128: j * D + (h + 1) * 128],
                                          start=(j == 0), stop=(j == i4))
                wk.inc(wk.T and ins, "s_mm")
                wk.w(wk.V, "s_mm")
                ins = wk.V and wk.V.scalar_tensor_tensor(
                    out=attn[:, i4 * D + h * 128: i4 * D + (h + 1) * 128],
                    in0=ps[7][:, 0:128], scalar=stf[:, 32 + 2 * i4:33 + 2 * i4],
                    in1=attn[:, i4 * D + h * 128: i4 * D + (h + 1) * 128],
                    op0=ALU.mult, op1=ALU.add)
                wk.inc(wk.V and ins, "s_ve")
                wk.bank_set(7, "s_ve")
            if DEBUG2 and s == 0 and h == 1:
                dbg_dump(wk, dbg_sc[:, 0:512], attn[:, 0:512])
            adot_mm_prev = wk.c["s_mm"]
            if DEBUG2 and s == 0 and h == 0:
                dbg_dump(wk, dbg_P[:, :], P_sb[:, :])
                dbg_dump(wk, dbg_st[:, :], stf[:, 32:48])
            widx = s * 8 + h
            if widx < len(WCH):
                w_emit_chunk(wk, widx)
        if DEBUG2 and s == 0:
            dbg_dump(wk, dbg_at[:, :], attn[:, :])
        for tc in range(4):
            wk.w(wk.V, "s_x")
            wk.w(wk.V, "s_ve")
            ins = wk.V and wk.V.tensor_tensor(out=attn[:, tc * D:(tc + 1) * D],
                                              in0=attn[:, tc * D:(tc + 1) * D],
                                              in1=xf[:, (tc % 2) * D:(tc % 2 + 1) * D], op=ALU.add)
            wk.inc(wk.V and ins, "s_ve")
            wk.w(wk.A, "s_y")
            ln_apply(wk, attn[:, tc * D:(tc + 1) * D], ybuf[:, :], ybuf[:, :])
            ins = wk.A and wk.A.dma_start(out=y_d[s * SEG + tc * 128: s * SEG + (tc + 1) * 128, :],
                                          in_=ybuf[:, :])
            wk.inc(ins, "s_y", 16)
            nseg, ntc = (s, tc + 2) if tc < 2 else (s + 1, tc - 2)
            if nseg < NSEG:
                ins = wk.A and wk.A.dma_start(out=xf[:, (tc % 2) * D:(tc % 2 + 1) * D],
                                              in_=x_d[nseg * SEG + ntc * 128: nseg * SEG + (ntc + 1) * 128, :])
                wk.inc(ins, "s_x", 16)

    def emit_phase_b(wk):
        pipe = QPipe(wk)
        barrier(wk)
        wk.w(wk.A, "s_ws")
        wk.w(wk.A, "s_y")
        ins = wk.A and wk.A.dma_start(out=wdq_sb[:, :], in_=wdq_d[:, :])
        wk.inc(ins, "s_pn", 16)

        def load_panels(nb):
            ndr = 4 if nb < 5 else 2
            w0 = nb * 32 * 128
            w1 = (nb * 32 + ndr * 8) * 128
            ins = wk.A and wk.A.dma_start(out=pnl_g[:, 0:w1 - w0], in_=wgq_d[:, w0:w1])
            wk.inc(ins, "s_pn", 16)
            ins = wk.A and wk.A.dma_start(out=pnl_u[:, 0:w1 - w0], in_=wuq_d[:, w0:w1])
            wk.inc(ins, "s_pn", 16)

        for t8 in range(NSEG):
            barrier(wk)
            for tc in range(4):
                ins = wk.A and wk.A.dma_start(out=y4[:, tc * D:(tc + 1) * D],
                                              in_=y_d[t8 * SEG + tc * 128: t8 * SEG + (tc + 1) * 128, :])
                wk.inc(ins, "s_yl", 16)
            wk.w(wk.V, "s_yl")
            for tc in range(4):
                for sl in range(2):
                    pipe.push(y4[:, tc * D + sl * 512: tc * D + (sl + 1) * 512],
                              xq[:, tc * D + sl * 512: tc * D + (sl + 1) * 512], 512)
            pipe.flush()
            wk.w(wk.S, "s_gp")
            ins = wk.S and wk.S.dma_start(out=xqT.rearrange("p (k f) -> p k f", f=128),
                                          in_=xq.rearrange("p (k f) -> p k f", f=128), transpose=True)
            wk.inc(ins, "s_tr", 16)
            tr_xq = wk.c["s_tr"]
            load_panels(0)
            for nb in range(6):
                wid = NBW[nb]
                ndr = 4 if nb < 5 else 2
                pn_now = wk.c["s_pn"]
                for tc in range(4):
                    bg = ps[tc % 2]
                    bu = ps[2 + tc % 2]
                    wk.w(wk.T, "s_tr", tr_xq)
                    wk.w(wk.T, "s_pn", pn_now)
                    wk.bank_wait(tc % 2)
                    wk.bank_wait(2 + tc % 2)
                    if wk.T:
                        gv = pnl_g[:, 0:ndr * 8 * 128].rearrange("p (dr j f) -> p j dr f", j=8, f=128)
                        uv = pnl_u[:, 0:ndr * 8 * 128].rearrange("p (dr j f) -> p j dr f", j=8, f=128)
                        for j in range(NC8):
                            wk.T.matmul(bg[:, 0:wid],
                                        xqT[:, (tc * 8 + j) * 128:(tc * 8 + j) * 128 + 128],
                                        gv[:, j, :, :], start=(j == 0), stop=(j == NC8 - 1))
                        for j in range(NC8):
                            ins = wk.T.matmul(bu[:, 0:wid],
                                              xqT[:, (tc * 8 + j) * 128:(tc * 8 + j) * 128 + 128],
                                              uv[:, j, :, :], start=(j == 0), stop=(j == NC8 - 1))
                    wk.inc(wk.T and ins, "s_mm")
                    wk.w(wk.A, "s_mm")
                    wk.w(wk.A, "s_ve", wk.silb_free)
                    if wk.A:
                        wk.A.activation(out=silb[:, 0:wid], in_=bg[:, 0:wid], func=AF.Exp, scale=-1.0)
                        wk.A.activation(out=silb[:, 0:wid], in_=silb[:, 0:wid], func=AF.Ln, bias=1.0)
                        ins = wk.A.activation(out=silb[:, 0:wid], in_=silb[:, 0:wid], func=AF.Exp, scale=-1.0)
                    wk.inc(wk.A and ins, "s_ev")
                    wk.w(wk.V, "s_ev")
                    par = tc % 2
                    ins = wk.V and wk.V.tensor_tensor(out=silb[:, 0:wid], in0=silb[:, 0:wid],
                                                      in1=bg[:, 0:wid], op=ALU.mult)
                    vf(wk, wk.V and ins)
                    ins = wk.V and wk.V.tensor_tensor(out=h_sl[:, par * 512: par * 512 + wid],
                                                      in0=silb[:, 0:wid], in1=bu[:, 0:wid], op=ALU.mult)
                    wk.inc(wk.V and ins, "s_ve")
                    wk.bank_set(tc % 2, "s_ve")
                    wk.bank_set(2 + tc % 2, "s_ve")
                    wk.silb_free = wk.c["s_ve"]
                    pipe.push(h_sl[:, par * 512: par * 512 + wid],
                              hq_full[:, tc * II + NBO[nb]: tc * II + NBO[nb] + wid], wid)
                if nb + 1 < 6:
                    wk.w(wk.A, "s_mm")
                    load_panels(nb + 1)
            pipe.flush()
            for tc in range(4):
                par = tc % 2
                wk.w(wk.S, "s_gp")
                wk.w(wk.S, "s_mm")
                ins = wk.S and wk.S.dma_start(
                    out=hqT[:, par * II: par * II + II].rearrange("p (k f) -> p k f", f=128),
                    in_=hq_full[:, tc * II:(tc + 1) * II].rearrange("p (k f) -> p k f", f=128),
                    transpose=True)
                wk.inc(ins, "s_tr", 16)
                for half in range(2):
                    bank = ps[4 + half]
                    wk.w(wk.T, "s_tr")
                    wk.w(wk.T, "s_pn")
                    wk.bank_wait(4 + half)
                    if wk.T:
                        wv_ = wdq_sb.rearrange("p (rn i f) -> p i rn f", rn=8, f=128)
                        for i22 in range(NI):
                            ins = wk.T.matmul(bank[:, 0:512],
                                              hqT[:, par * II + i22 * 128: par * II + (i22 + 1) * 128],
                                              wv_[:, i22, half * 4:(half + 1) * 4, :],
                                              start=(i22 == 0), stop=(i22 == NI - 1))
                    wk.inc(wk.T and ins, "s_mm")
                    wk.w(wk.V, "s_mm")
                    wk.w(wk.V, "s_ev")
                    ins = wk.V and wk.V.tensor_tensor(out=r2b[:, half * 512:(half + 1) * 512],
                                                      in0=bank[:, 0:512],
                                                      in1=y4[:, tc * D + half * 512: tc * D + (half + 1) * 512],
                                                      op=ALU.add)
                    wk.inc(wk.V and ins, "s_ve")
                    wk.bank_set(4 + half, "s_ve")
                wk.w(wk.A, "s_o")
                ln_apply(wk, r2b[:, :], obuf[:, :], obuf[:, :])
                ins = wk.A and wk.A.dma_start(out=out_d[t8 * SEG + tc * 128: t8 * SEG + (tc + 1) * 128, :],
                                              in_=obuf[:, :])
                wk.inc(ins, "s_o", 16)

    def emit_all(wk):
        emit_startup(wk)
        for s in range(NSEG):
            emit_segment(wk, s)
        emit_phase_b(wk)
        nst = 32 + (11 if DEBUG2 else 0)
        if DEBUG_Y:
            wk.w(wk.G, "s_y", 16 * 32)
            ins = wk.G and wk.G.dma_start(out=ydbg_d[:, :], in_=y_d[:, :])
            wk.inc(ins, "s_o", 16)
            nst += 1
        wk.w(wk.S, "s_o", 16 * nst)
        wk.w(wk.V, "s_o", 16 * nst)

    with nc.Block() as block:
        @block.sync
        def _(h):
            emit_all(Wk("sync", h))

        @block.scalar
        def _(h):
            emit_all(Wk("scalar", h))

        @block.vector
        def _(h):
            emit_all(Wk("vector", h))

        @block.tensor
        def _(h):
            emit_all(Wk("tensor", h))

        @block.gpsimd
        def _(h):
            emit_all(Wk("gpsimd", h))

    return nc


_NC_CACHE = []


def kernel(**inputs):
    f = lambda k: np.ascontiguousarray(np.asarray(inputs[k], dtype=np.float32))
    x = f("L0_attn_x")
    wq = f("L0_attn_w_q"); wkk = f("L0_attn_w_k"); wv = f("L0_attn_w_v")
    bq = f("L0_attn_b_q"); bk = f("L0_attn_b_k"); bv = f("L0_attn_b_v")
    beta = f("L0_attn_beta")
    wg = f("L0_ffn_w_gate"); wu = f("L0_ffn_w_up"); wd = f("L0_ffn_w_down")

    bq8 = np.ascontiguousarray(bq.reshape(8, 128).T)
    bk8 = np.ascontiguousarray(bk.reshape(8, 128).T)
    bvr = np.ascontiguousarray(bv.reshape(1, D))
    betar = np.ascontiguousarray(beta.reshape(1, NH))
    trilc = np.ascontiguousarray(np.tril(np.ones((128, 128), np.float32)).astype(ml_dtypes.bfloat16))

    if not _NC_CACHE:
        _NC_CACHE.append(build_program())
    nc = _NC_CACHE[0]
    common = dict(wq=wq, wk=wkk, wv=wv, bq8=bq8, bk8=bk8, bvrow=bvr, beta=betar,
                  wg=wg, wu=wu, wd=wd, tril=trilc)
    in_maps = [dict(common, x=np.ascontiguousarray(x[b])) for b in range(8)]
    res = run_bass_kernel_spmd(nc, in_maps, core_ids=list(range(8)))
    out = np.stack([np.asarray(r["out"]) for r in res.results], axis=0)
    return out.astype(np.float32)


# version 12
# speedup vs baseline: 1.0681x; 1.0681x over previous
import numpy as np
import ml_dtypes
import concourse.bass as bass
import concourse.mybir as mybir
from concourse.bass_utils import run_bass_kernel_spmd

F32 = mybir.dt.float32
BF16 = mybir.dt.bfloat16
AF = mybir.ActivationFunctionType
ALU = mybir.AluOpType
AX = mybir.AxisListType

D = 1024
T = 4096
SEG = 512
NSEG = 8
NH = 8
DH = 128
II = 2816
NI = 22
NC8 = 8
ISD = float(1.0 / np.sqrt(np.float32(128.0)))
EPS = 1e-6
LNEPS = 1e-5
NBW = [512, 512, 512, 512, 512, 256]
NBO = [0, 512, 1024, 1536, 2048, 2560]

DEBUG_Y = False
DEBUG2 = False
SEMS = ["s_vf", "s_ld", "s_x", "s_xb", "s_tr", "s_y", "s_yl", "s_pn", "s_wc", "s_ws", "s_o",
        "s_mm", "s_ev", "s_ve", "s_gp"]


def build_program():
    nc = bass.Bass("TRN2", target_bir_lowering=False)

    x_d = nc.dram_tensor("x", [T, D], F32, kind="ExternalInput").ap()
    wq_d = nc.dram_tensor("wq", [D, D], F32, kind="ExternalInput").ap()
    wk_d = nc.dram_tensor("wk", [D, D], F32, kind="ExternalInput").ap()
    wv_d = nc.dram_tensor("wv", [D, D], F32, kind="ExternalInput").ap()
    bq_d = nc.dram_tensor("bq8", [128, 8], F32, kind="ExternalInput").ap()
    bk_d = nc.dram_tensor("bk8", [128, 8], F32, kind="ExternalInput").ap()
    bv_d = nc.dram_tensor("bvrow", [1, D], F32, kind="ExternalInput").ap()
    beta_d = nc.dram_tensor("beta", [1, NH], F32, kind="ExternalInput").ap()
    wg_d = nc.dram_tensor("wg", [II, D], F32, kind="ExternalInput").ap()
    wu_d = nc.dram_tensor("wu", [II, D], F32, kind="ExternalInput").ap()
    wd_d = nc.dram_tensor("wd", [D, II], F32, kind="ExternalInput").ap()
    tril_d = nc.dram_tensor("tril", [128, 128], BF16, kind="ExternalInput").ap()
    out_d = nc.dram_tensor("out", [T, D], F32, kind="ExternalOutput").ap()
    ydbg_d = nc.dram_tensor("ydbg", [T, D], F32, kind="ExternalOutput").ap() if DEBUG_Y else None
    if DEBUG2:
        dbg_xT = nc.dram_tensor("dbg_xT", [128, 4096], BF16, kind="ExternalOutput").ap()
        dbg_qT = nc.dram_tensor("dbg_qT", [128, 4096], BF16, kind="ExternalOutput").ap()
        dbg_kT = nc.dram_tensor("dbg_kT", [128, 4096], BF16, kind="ExternalOutput").ap()
        dbg_sq = nc.dram_tensor("dbg_sq", [128, 4096], BF16, kind="ExternalOutput").ap()
        dbg_v = nc.dram_tensor("dbg_v", [128, 4096], BF16, kind="ExternalOutput").ap()
        dbg_at = nc.dram_tensor("dbg_at", [128, 4096], F32, kind="ExternalOutput").ap()
        dbg_P = nc.dram_tensor("dbg_P", [128, 1280], BF16, kind="ExternalOutput").ap()
        dbg_st = nc.dram_tensor("dbg_st", [128, 16], F32, kind="ExternalOutput").ap()
        dbg_st2 = nc.dram_tensor("dbg_st2", [128, 32], F32, kind="ExternalOutput").ap()
        dbg_am = nc.dram_tensor("dbg_am", [128, 128], F32, kind="ExternalOutput").ap()
        dbg_sc = nc.dram_tensor("dbg_sc", [128, 512], F32, kind="ExternalOutput").ap()

    y_d = nc.dram_tensor("yscratch", [T, D], F32, kind="Internal").ap()
    wgq_d = nc.dram_tensor("wgq", [128, NI * 8 * 128], BF16, kind="Internal").ap()
    wuq_d = nc.dram_tensor("wuq", [128, NI * 8 * 128], BF16, kind="Internal").ap()
    wdq_d = nc.dram_tensor("wdq", [128, 8 * NI * 128], BF16, kind="Internal").ap()

    SA = lambda name, shape, dt=F32, **kw: nc.alloc_sbuf_tensor(name, shape, dt, **kw).ap()
    tril = SA("trilsb", [128, 128], BF16)
    onesrow = SA("onesrow", [1, 128], BF16)
    bq8 = SA("bq8s", [128, 8])
    bk8 = SA("bk8s", [128, 8])
    bvrow = SA("bvrows", [1, D], BF16)
    beta_r = SA("betar", [1, NH])
    grow = SA("grow", [1, NH])
    growbf = SA("growbf", [1, NH], BF16)
    g8 = SA("g8", [128, NH])
    gm8 = SA("gm8", [128, NH])
    stf = SA("stf", [128, 128])
    rk_tmp = SA("rktmp", [128, 128])
    thr = SA("thr", [128, 8])
    mnb = SA("mnb", [128, 512], BF16)
    exb = SA("exb", [128, 512], BF16)

    base = nc._sbuf_addr_for_side("left")
    base = (base + 511) & ~511
    cur = [base]

    def AR(name, shape, dt):
        nbytes = int(np.prod(shape[1:])) * (2 if dt == BF16 else 4)
        off = cur[0]
        h = nc.alloc_sbuf_tensor_at(name, shape, dt, offset=off)
        cur[0] = (off + nbytes + 63) & ~63
        return h.ap()

    a0 = cur[0]
    wq_sb = AR("wq_sb", [128, 8 * D], BF16)
    wk_sb = AR("wk_sb", [128, 8 * D], BF16)
    wv_sb = AR("wv_sb", [128, 8 * D], BF16)
    Mf = AR("Mhatf", [128, NH * 132], F32)
    Mb = AR("Mhatb", [128, NH * 132], BF16)
    xf = AR("xf", [128, 2 * D], F32)
    xbf = AR("xbf", [128, 4 * D], BF16)
    xT = AR("xT", [128, 4 * D], BF16)
    qT = AR("qT", [128, NH * 512], BF16)
    kT = AR("kT", [128, NH * 512], BF16)
    sqT = AR("sqT", [128, NH * 512], BF16)
    skT = AR("skT", [128, NH * 512], BF16)
    sk_tok = AR("sk_tok", [128, NH * 512], BF16)
    vbf = AR("vbf", [128, 4 * D], BF16)
    P_sb = AR("P_sb", [128, 1280], BF16)
    P_sb2 = AR("P_sb2", [128, 1280], BF16)
    PT_sb = AR("PT_sb", [128, 1280], BF16)
    u_sb = AR("u_sb", [128, 4 * 136], BF16)
    attn = AR("attn", [128, 4 * D], F32)
    ybuf = AR("ybuf", [128, D], F32)
    a_end = cur[0]

    cur[0] = a0
    y4 = AR("y4", [128, 4 * D], F32)
    xq = AR("xq", [128, 4 * D], BF16)
    xqT = AR("xqT", [128, 4 * D], BF16)
    pnl_g = AR("pnl_g", [128, 4 * D], BF16)
    pnl_u = AR("pnl_u", [128, 4 * D], BF16)
    wdq_sb = AR("wdq_sb", [128, 8 * NI * 128], BF16)
    hq_full = AR("hq_full", [128, 4 * II], BF16)
    hqT = AR("hqT", [128, 2 * II], BF16)
    h_sl = AR("h_sl", [128, 2 * 512], F32)
    silb = AR("silb", [128, 512], F32)
    r2b = AR("r2b", [128, D], F32)
    obuf = AR("obuf", [128, D], F32)
    b_end = cur[0]

    cur[0] = (b_end + 63) & ~63
    qt_u2 = AR("qt_u2", [128, 512], F32)
    s1b2 = AR("s1b2", [128, 512], BF16)
    s3b2 = AR("s3b2", [128, 512], BF16)
    s5b2 = AR("s5b2", [128, 512], BF16)
    s10b2 = AR("s10b2", [128, 512], BF16)
    s14b2 = AR("s14b2", [128, 512], BF16)
    sgb2 = AR("sgb2", [128, 512], BF16)
    wSh2 = AR("wSh2", [128, 32], BF16)
    assert cur[0] <= a_end, f"shadow overflow {cur[0]} > {a_end}"

    cur[0] = (max(a_end, b_end) + 511) & ~511
    wpc = AR("wpc", [128, 2 * 1024], F32)
    qslc = AR("qslc", [128, 512], BF16)
    ttmp = AR("ttmp", [128, II], BF16)
    qt_uh = AR("qt_uh", [128, 512], F32)
    qt_u = AR("qt_u", [128, 512], F32)
    s1b = AR("s1b", [128, 512], BF16)
    s3b = AR("s3b", [128, 512], BF16)
    s5b = AR("s5b", [128, 512], BF16)
    s10b = AR("s10b", [128, 512], BF16)
    s14b = AR("s14b", [128, 512], BF16)
    sgb = AR("sgb", [128, 512], BF16)
    c7b = AR("c7b", [128, 512], BF16)
    c20b = AR("c20b", [128, 512], BF16)
    m1b = AR("m1b", [128, 512], BF16)
    wm_ = AR("wm_", [128, 32], F32)
    wS_ = AR("wS_", [128, 32], F32)
    wR_ = AR("wR_", [128, 32], F32)
    wSh = AR("wSh", [128, 32], BF16)
    w_end = cur[0]
    assert w_end <= 212000, f"sbuf overflow: {w_end}"
    import os
    if os.environ.get("PRINT_SBUF"): print("w_end:", w_end, "slack:", 212000 - w_end, "a_end:", a_end, "b_end:", b_end)

    ps = [nc.alloc_psum_tensor(f"ps{i}", [128, 512], F32).ap() for i in range(8)]

    sems = {k: nc.alloc_semaphore(k) for k in SEMS}

    class Wk:
        def __init__(self, name, h):
            self.name = name
            self.T = h if name == "tensor" else None
            self.V = h if name == "vector" else None
            self.A = h if name == "scalar" else None
            self.G = h if name == "gpsimd" else None
            self.S = h if name == "sync" else None
            self.c = {k: 0 for k in SEMS}
            self.bank = {}
            self.bias_free = 0
            self.wq = 0
            self.silb_free = 0

        def __post(self):
            pass

        def inc(self, ins, sem, k=1):
            self.c[sem] += k
            if ins is not None:
                ins.then_inc(sems[sem], k)

        def bank_set(self, b, sem):
            self.bank[b] = (sem, self.c[sem])

        def bank_wait(self, b):
            if b in self.bank:
                sem, v = self.bank[b]
                self.w(self.T, sem, v)

        def w(self, h, sem, val=None):
            if h is not None:
                h.wait_ge(sems[sem], self.c[sem] if val is None else val)

    def vf(wk, ins):
        wk.inc(ins, "s_vf")
        wk.w(wk.V, "s_vf")

    def bc16(small, nb16):
        return small[:, 0:nb16].unsqueeze(2).broadcast_to([128, nb16, 16])


    def quant_slice(wk, src, dst, n):
        nb16 = n // 16
        v3 = lambda ap: ap.rearrange("p (k f) -> p k f", f=16)
        wk.w(wk.V, "s_ev")
        wk.w(wk.V, "s_gp")
        wk.w(wk.V, "s_ve")
        ins = wk.V and wk.V.tensor_reduce(out=wm_[:, 0:nb16], in_=v3(src), axis=AX.X,
                                          op=ALU.max, apply_absolute_value=True)
        vf(wk, wk.V and ins)
        ins = wk.V and wk.V.tensor_scalar(out=wS_[:, 0:nb16], in0=wm_[:, 0:nb16], scalar1=float(1.0 / 6.0),
                                          scalar2=1e-12, op0=ALU.mult, op1=ALU.max)
        wk.inc(wk.V and ins, "s_ve")
        wk.w(wk.A, "s_ve")
        if wk.A:
            wk.A.activation(out=wR_[:, 0:nb16], in_=wS_[:, 0:nb16], func=AF.Ln)
            wk.A.activation(out=wR_[:, 0:nb16], in_=wR_[:, 0:nb16], func=AF.Exp, scale=-1.0)
            wk.A.activation(out=qt_uh[:, 0:n], in_=src, func=AF.Abs)
            ins = wk.A.activation(out=sgb[:, 0:n], in_=src, func=AF.Sign)
        wk.inc(wk.A and ins, "s_ev")
        wk.w(wk.V, "s_ev")
        if wk.V:
            wk.V.tensor_scalar(out=wSh[:, 0:nb16], in0=wS_[:, 0:nb16], scalar1=0.5,
                               scalar2=None, op0=ALU.mult)
            ins = wk.V.scalar_tensor_tensor(out=v3(qt_u[:, 0:n]), in0=v3(qt_uh[:, 0:n]), scalar=1.0,
                                            in1=bc16(wR_, nb16), op0=ALU.mult, op1=ALU.mult)
        wk.inc(wk.V and ins, "s_ve")
        wk.w(wk.A, "s_ve")
        if wk.A:
            wk.A.activation(out=s1b[:, 0:n], in_=qt_u[:, 0:n], func=AF.Sign, bias=thr[:, 0:1])
            wk.A.activation(out=s3b[:, 0:n], in_=qt_u[:, 0:n], func=AF.Sign, bias=thr[:, 1:2])
            wk.A.activation(out=s5b[:, 0:n], in_=qt_u[:, 0:n], func=AF.Sign, bias=thr[:, 2:3])
            wk.A.activation(out=s10b[:, 0:n], in_=qt_u[:, 0:n], func=AF.Sign, bias=thr[:, 3:4])
            ins = wk.A.activation(out=s14b[:, 0:n], in_=qt_u[:, 0:n], func=AF.Sign, bias=thr[:, 4:5])
        wk.inc(wk.A and ins, "s_ev")
        wk.w(wk.V, "s_ev")
        if wk.V:
            wk.V.tensor_scalar(out=c7b[:, 0:n], in0=qt_u[:, 0:n], scalar1=1.75, scalar2=None, op0=ALU.is_gt)
            wk.V.tensor_scalar(out=c20b[:, 0:n], in0=qt_u[:, 0:n], scalar1=5.0, scalar2=2.0,
                               op0=ALU.is_gt, op1=ALU.mult)
            wk.V.tensor_tensor(out=s1b[:, 0:n], in0=s1b[:, 0:n], in1=s3b[:, 0:n], op=ALU.add)
            wk.V.tensor_tensor(out=s1b[:, 0:n], in0=s1b[:, 0:n], in1=s5b[:, 0:n], op=ALU.add)
            wk.V.tensor_tensor(out=s10b[:, 0:n], in0=s10b[:, 0:n], in1=s14b[:, 0:n], op=ALU.add)
            wk.V.scalar_tensor_tensor(out=s1b[:, 0:n], in0=s1b[:, 0:n], scalar=0.5, in1=c7b[:, 0:n],
                                      op0=ALU.mult, op1=ALU.add)
            wk.V.tensor_tensor(out=s1b[:, 0:n], in0=s1b[:, 0:n], in1=s10b[:, 0:n], op=ALU.add)
            wk.V.scalar_tensor_tensor(out=s1b[:, 0:n], in0=c20b[:, 0:n], scalar=2.0, in1=s1b[:, 0:n],
                                      op0=ALU.mult, op1=ALU.add)
            ins = wk.V.scalar_tensor_tensor(out=m1b[:, 0:n], in0=s1b[:, 0:n], scalar=3.5, in1=sgb[:, 0:n],
                                            op0=ALU.add, op1=ALU.mult)
        else:
            ins = None
        wk.inc(wk.V and ins, "s_ve")
        wk.w(wk.G, "s_ve")
        ins = wk.G and wk.G.tensor_tensor(out=v3(dst), in0=v3(m1b[:, 0:n]),
                                          in1=bc16(wSh, nb16), op=ALU.mult)
        wk.inc(ins, "s_gp")

    QT = [dict(qt_u=qt_u, s1b=s1b, s3b=s3b, s5b=s5b, s10b=s10b, s14b=s14b, sgb=sgb, wSh=wSh),
          dict(qt_u=qt_u2, s1b=s1b2, s3b=s3b2, s5b=s5b2, s10b=s10b2, s14b=s14b2, sgb=sgb2, wSh=wSh2)]

    class QPipe:

        def __init__(self, wk):
            self.wk = wk
            self.k = 0
            self.pending = None
            self.s5_done = [0, 0]
            self.s6_done = [0, 0]
            self.s4_done = [0, 0]
            self.s2_done = 0
            self.s3_done = 0

        def push(self, src, dst, n):
            wk = self.wk
            par = self.k % 2
            t = QT[par]
            nb16 = n // 16
            v3 = lambda ap: ap[:, 0:n].rearrange("p (kk f) -> p kk f", f=16)
            wk.w(wk.V, "s_ev", self.s2_done)
            wk.w(wk.V, "s_ve")
            ins = wk.V and wk.V.tensor_reduce(out=wm_[:, 0:nb16], in_=src.rearrange("p (kk f) -> p kk f", f=16),
                                              axis=AX.X, op=ALU.max, apply_absolute_value=True)
            vf(wk, wk.V and ins)
            ins = wk.V and wk.V.tensor_scalar(out=wS_[:, 0:nb16], in0=wm_[:, 0:nb16], scalar1=float(1.0 / 6.0),
                                              scalar2=1e-12, op0=ALU.mult, op1=ALU.max)
            wk.inc(wk.V and ins, "s_ve")
            wk.w(wk.A, "s_ve", self.s3_done)
            wk.w(wk.A, "s_ve", self.s5_done[par])
            wk.w(wk.A, "s_ve")
            if wk.A:
                wk.A.activation(out=wR_[:, 0:nb16], in_=wS_[:, 0:nb16], func=AF.Ln)
                wk.A.activation(out=wR_[:, 0:nb16], in_=wR_[:, 0:nb16], func=AF.Exp, scale=-1.0)
                wk.A.activation(out=qt_uh[:, 0:n], in_=src, func=AF.Abs)
                ins = wk.A.activation(out=t["sgb"][:, 0:n], in_=src, func=AF.Sign)
            wk.inc(wk.A and ins, "s_ev")
            self.s2_done = wk.c["s_ev"]
            wk.w(wk.V, "s_ev", self.s4_done[par])
            wk.w(wk.V, "s_gp", self.s6_done[par])
            wk.w(wk.V, "s_ev", self.s2_done)
            if wk.V:
                wk.V.tensor_scalar(out=t["wSh"][:, 0:nb16], in0=wS_[:, 0:nb16], scalar1=0.5,
                                   scalar2=None, op0=ALU.mult)
                ins = wk.V.scalar_tensor_tensor(out=v3(t["qt_u"]), in0=v3(qt_uh), scalar=1.0,
                                                in1=bc16(wR_, nb16), op0=ALU.mult, op1=ALU.mult)
            wk.inc(wk.V and ins, "s_ve")
            self.s3_done = wk.c["s_ve"]
            wk.w(wk.A, "s_ve", self.s3_done)
            if wk.A:
                wk.A.activation(out=t["s1b"][:, 0:n], in_=t["qt_u"][:, 0:n], func=AF.Sign, bias=thr[:, 0:1])
                wk.A.activation(out=t["s3b"][:, 0:n], in_=t["qt_u"][:, 0:n], func=AF.Sign, bias=thr[:, 1:2])
                wk.A.activation(out=t["s5b"][:, 0:n], in_=t["qt_u"][:, 0:n], func=AF.Sign, bias=thr[:, 2:3])
                wk.A.activation(out=t["s10b"][:, 0:n], in_=t["qt_u"][:, 0:n], func=AF.Sign, bias=thr[:, 3:4])
                ins = wk.A.activation(out=t["s14b"][:, 0:n], in_=t["qt_u"][:, 0:n], func=AF.Sign, bias=thr[:, 4:5])
            wk.inc(wk.A and ins, "s_ev")
            self.s4_done[par] = wk.c["s_ev"]
            prev = self.pending
            self.pending = (dst, n, par, self.s4_done[par])
            self.k += 1
            if prev is not None:
                self._emit_tail(prev)

        def flush(self):
            if self.pending is not None:
                self._emit_tail(self.pending)
                self.pending = None

        def _emit_tail(self, job):
            wk = self.wk
            dst, n, par, s4cnt = job
            t = QT[par]
            nb16 = n // 16
            v3 = lambda ap: ap[:, 0:n].rearrange("p (kk f) -> p kk f", f=16)
            wk.w(wk.V, "s_ev", s4cnt)
            wk.w(wk.V, "s_gp", self.s6_done[1 - par])
            if wk.V:
                wk.V.tensor_scalar(out=c7b[:, 0:n], in0=t["qt_u"][:, 0:n], scalar1=1.75, scalar2=None, op0=ALU.is_gt)
                wk.V.tensor_scalar(out=c20b[:, 0:n], in0=t["qt_u"][:, 0:n], scalar1=5.0, scalar2=2.0,
                                   op0=ALU.is_gt, op1=ALU.mult)
                wk.V.tensor_tensor(out=t["s1b"][:, 0:n], in0=t["s1b"][:, 0:n], in1=t["s3b"][:, 0:n], op=ALU.add)
                wk.V.tensor_tensor(out=t["s1b"][:, 0:n], in0=t["s1b"][:, 0:n], in1=t["s5b"][:, 0:n], op=ALU.add)
                wk.V.tensor_tensor(out=t["s10b"][:, 0:n], in0=t["s10b"][:, 0:n], in1=t["s14b"][:, 0:n], op=ALU.add)
                wk.V.scalar_tensor_tensor(out=t["s1b"][:, 0:n], in0=t["s1b"][:, 0:n], scalar=0.5,
                                          in1=c7b[:, 0:n], op0=ALU.mult, op1=ALU.add)
                wk.V.tensor_tensor(out=t["s1b"][:, 0:n], in0=t["s1b"][:, 0:n], in1=t["s10b"][:, 0:n], op=ALU.add)
                wk.V.scalar_tensor_tensor(out=t["s1b"][:, 0:n], in0=c20b[:, 0:n], scalar=2.0,
                                          in1=t["s1b"][:, 0:n], op0=ALU.mult, op1=ALU.add)
                ins = wk.V.scalar_tensor_tensor(out=m1b[:, 0:n], in0=t["s1b"][:, 0:n], scalar=3.5,
                                                in1=t["sgb"][:, 0:n], op0=ALU.add, op1=ALU.mult)
            else:
                ins = None
            wk.inc(wk.V and ins, "s_ve")
            self.s5_done[par] = wk.c["s_ve"]
            wk.w(wk.G, "s_ve", self.s5_done[par])
            ins = wk.G and wk.G.tensor_tensor(out=v3(dst), in0=v3(m1b), in1=bc16(t["wSh"], nb16), op=ALU.mult)
            wk.inc(ins, "s_gp")
            self.s6_done[par] = wk.c["s_gp"]

    WCH = ([("g", r, 0, D, True, D) for r in range(NI)]
           + [("u", r, 0, D, True, D) for r in range(NI)])
    for rn in range(8):
        WCH.append(("d", rn, 0, 1024, False, II))
        WCH.append(("d", rn, 1024, 1024, False, II))
        WCH.append(("d", rn, 2048, 768, True, II))

    def w_emit_load(wk, idx):
        kind, r, off, wid, last, F = WCH[idx]
        src = {"g": wg_d, "u": wu_d, "d": wd_d}[kind]
        buf = wpc[:, (idx % 2) * 1024: (idx % 2) * 1024 + wid]
        ins = wk.A and wk.A.dma_start(out=buf, in_=src[r * 128:(r + 1) * 128, off:off + wid])
        wk.inc(ins, "s_wc", 16)

    def w_emit_chunk(wk, idx):
        kind, r, off, wid, last, F = WCH[idx]
        buf = wpc[:, (idx % 2) * 1024:]
        wk.w(wk.V, "s_wc")
        pos = 0
        while pos < wid:
            n = min(512, wid - pos)
            quant_slice(wk, buf[:, pos:pos + n], qslc[:, 0:n], n)
            j0 = (off + pos) // 128
            nt = n // 128
            wk.w(wk.S, "s_gp")
            wk.w(wk.S, "s_ws")
            ins = wk.S and wk.S.dma_start(
                out=ttmp[:, j0 * 128: (j0 + nt) * 128].rearrange("p (k f) -> p k f", f=128),
                in_=qslc[:, 0:n].rearrange("p (k f) -> p k f", f=128), transpose=True)
            wk.inc(ins, "s_tr", 16)
            pos += n
        if idx + 1 < len(WCH):
            wk.w(wk.A, "s_ve")
            w_emit_load(wk, idx + 1)
        if last:
            wk.w(wk.A, "s_tr")
            dstt = {"g": wgq_d, "u": wuq_d, "d": wdq_d}[kind]
            ins = wk.A and wk.A.dma_start(out=dstt[:, r * F: (r + 1) * F], in_=ttmp[:, 0:F])
            wk.inc(ins, "s_ws", 16)

    def barrier(wk):
        for h in (wk.T, wk.V, wk.A, wk.G, wk.S):
            if h is not None:
                for s in ("s_mm", "s_ev", "s_ve", "s_gp"):
                    wk.w(h, s)
        wk.w(wk.V, "s_ve")

    def dbg_dump(wk, dst, src):
        for sem in ("s_mm", "s_ev", "s_ve", "s_gp"):
            wk.w(wk.G, sem)
        ins = wk.G and wk.G.dma_start(out=dst, in_=src)
        wk.inc(ins, "s_o", 16)
        for h in (wk.T, wk.V, wk.A, wk.S):
            wk.w(h, "s_o")

    def ln_apply(wk, src, dst, trash):
        wk.w(wk.V, "s_ve")
        ins = wk.V and wk.V.tensor_reduce(out=stf[:, 40:41], in_=src, axis=AX.X, op=ALU.add)
        wk.inc(wk.V and ins, "s_ve")
        wk.w(wk.A, "s_ve")
        ins = wk.A and wk.A.activation(out=trash, in_=src, func=AF.Square, accum_out=stf[:, 42:43])
        wk.inc(ins, "s_ev")
        wk.w(wk.V, "s_ev")
        if wk.V:
            wk.V.tensor_scalar(out=stf[:, 44:45], in0=stf[:, 40:41], scalar1=float(1.0 / D),
                               scalar2=None, op0=ALU.mult)
            ins = wk.V.tensor_scalar(out=stf[:, 46:47], in0=stf[:, 42:43], scalar1=float(1.0 / D),
                                     scalar2=None, op0=ALU.mult)
        else:
            ins = None
        vf(wk, wk.V and ins)
        ins = wk.V and wk.V.tensor_tensor(out=stf[:, 48:49], in0=stf[:, 44:45], in1=stf[:, 44:45], op=ALU.mult)
        vf(wk, wk.V and ins)
        ins = wk.V and wk.V.tensor_tensor(out=stf[:, 50:51], in0=stf[:, 46:47], in1=stf[:, 48:49], op=ALU.subtract)
        vf(wk, wk.V and ins)
        ins = wk.V and wk.V.tensor_scalar(out=stf[:, 52:53], in0=stf[:, 50:51], scalar1=LNEPS,
                                          scalar2=None, op0=ALU.add)
        wk.inc(wk.V and ins, "s_ve")
        wk.w(wk.A, "s_ve")
        if wk.A:
            wk.A.activation(out=stf[:, 54:55], in_=stf[:, 52:53], func=AF.Ln)
            ins = wk.A.activation(out=stf[:, 56:57], in_=stf[:, 54:55], func=AF.Exp, scale=-0.5)
        wk.inc(wk.A and ins, "s_ev")
        wk.w(wk.V, "s_ev")
        ins = wk.V and wk.V.tensor_tensor(out=stf[:, 58:59], in0=stf[:, 44:45], in1=stf[:, 56:57], op=ALU.mult)
        vf(wk, wk.V and ins)
        ins = wk.V and wk.V.tensor_scalar(out=stf[:, 60:61], in0=stf[:, 58:59], scalar1=-1.0,
                                          scalar2=None, op0=ALU.mult)
        wk.inc(wk.V and ins, "s_ve")
        wk.w(wk.A, "s_ve")
        ins = wk.A and wk.A.activation(out=dst, in_=src, func=AF.Identity, scale=stf[:, 56:57], bias=stf[:, 60:61])
        wk.inc(ins, "s_ev")

    def emit_startup(wk):
        if wk.A:
            wk.A.add_instruction(mybir.InstLoadActFuncSet(
                name=nc.get_next_instruction_name(), act_func_set_id=6, ins=[], outs=[]))
        for dst, src in ((tril, tril_d), (bq8, bq_d), (bk8, bk_d), (beta_r, beta_d)):
            ins = wk.A and wk.A.dma_start(out=dst[:, :], in_=src[:, :])
            wk.inc(ins, "s_ld", 16)
        for tc in range(2):
            ins = wk.A and wk.A.dma_start(out=xf[:, tc * D:(tc + 1) * D],
                                          in_=x_d[tc * 128:(tc + 1) * 128, :])
            wk.inc(ins, "s_x", 16)
        for dst, src in ((wq_sb, wq_d), (wk_sb, wk_d), (wv_sb, wv_d)):
            ins = wk.G and wk.G.dma_start(out=dst.rearrange("p (j n) -> p j n", n=D),
                                          in_=src.rearrange("(j p) n -> p j n", p=128))
            wk.inc(ins, "s_ld", 16)
        ins = wk.G and wk.G.dma_start(out=bvrow[:, :], in_=bv_d[:, :])
        wk.inc(ins, "s_ld", 16)
        ins = wk.G and wk.G.dma_start(out=xbf.rearrange("p (tc c) -> p tc c", c=D),
                                      in_=x_d[0:SEG, :].rearrange("(tc p) c -> p tc c", p=128))
        wk.inc(ins, "s_xb", 16)
        if wk.V:
            for col, val in ((0, -0.25), (1, -0.75), (2, -1.25), (3, -2.5), (4, -3.5)):
                wk.V.memset(thr[:, col:col + 1], val)
            wk.V.memset(Mf[:, :], 0.0)
            wk.V.memset(Mb[:, :], 0.0)
            wk.V.memset(onesrow[:, :], 1.0)
            for tc in range(4):
                ins = wk.V.memset(u_sb[:, tc * 136 + 128: tc * 136 + 129], 1.0)
        wk.inc(wk.V and ins, "s_ve")
        wk.w(wk.A, "s_ld", 16 * 4)
        if wk.A:
            wk.A.activation(out=grow[:, :], in_=beta_r[:, :], func=AF.Exp, scale=-1.0)
            wk.A.activation(out=grow[:, :], in_=grow[:, :], func=AF.Ln, bias=1.0)
            wk.A.activation(out=grow[:, :], in_=grow[:, :], func=AF.Exp, scale=-1.0)
            ins = wk.A.copy(out=growbf[:, :], in_=grow[:, :])
        wk.inc(wk.A and ins, "s_ev")
        wk.w(wk.T, "s_ev")
        wk.w(wk.T, "s_ve")
        ins = wk.T and wk.T.matmul(ps[7][:, 0:NH], onesrow[0:1, :], growbf[0:1, :], start=True, stop=True)
        wk.inc(ins, "s_mm")
        wk.w(wk.V, "s_mm")
        ins = wk.V and wk.V.tensor_copy(out=g8[:, :], in_=ps[7][:, 0:NH])
        vf(wk, wk.V and ins)
        ins = wk.V and wk.V.tensor_scalar(out=gm8[:, :], in0=g8[:, :], scalar1=-1.0, scalar2=1.0,
                                          op0=ALU.mult, op1=ALU.add)
        wk.inc(wk.V and ins, "s_ve")
        w_emit_load(wk, 0)

    def proj_qk(wk, which):
        w_sb = wq_sb if which == "q" else wk_sb
        b8 = bq8 if which == "q" else bk8
        oT = qT if which == "q" else kT
        oS = sqT if which == "q" else skT
        xTr = xT.rearrange("p (tc j2 f) -> p j2 tc f", j2=8, f=128)
        for h in range(NH):
            bi = h % 2
            bank = ps[bi]
            wk.bank_wait(bi)
            if wk.T:
                for j in range(NC8):
                    ins = wk.T.matmul(bank[:, 0:512],
                                      w_sb[:, j * D + h * 128: j * D + h * 128 + 128],
                                      xTr[:, j, :, :],
                                      start=(j == 0), stop=(j == NC8 - 1))
            wk.inc(wk.T and ins, "s_mm")
            wk.w(wk.A, "s_mm")
            ins = wk.A and wk.A.activation(out=oT[:, h * 512:(h + 1) * 512], in_=bank[:, 0:512],
                                           func=AF.Identity, bias=b8[:, h:h + 1])
            wk.inc(ins, "s_ev")
            wk.bank_set(bi, "s_ev")
            wk.w(wk.V, "s_ev")
            ins = wk.V and wk.V.tensor_scalar(out=mnb[:, :], in0=oT[:, h * 512:(h + 1) * 512],
                                              scalar1=0.0, scalar2=None, op0=ALU.min)
            wk.inc(wk.V and ins, "s_ve")
            wk.w(wk.A, "s_ve")
            ins = wk.A and wk.A.activation(out=exb[:, :], in_=mnb[:, :], func=AF.Exp)
            wk.inc(wk.A and ins, "s_ev")
            wk.w(wk.V, "s_ev")
            if wk.V:
                wk.V.tensor_scalar(out=oS[:, h * 512:(h + 1) * 512], in0=oT[:, h * 512:(h + 1) * 512],
                                   scalar1=0.0, scalar2=None, op0=ALU.max)
                ins = wk.V.tensor_tensor(out=oS[:, h * 512:(h + 1) * 512],
                                         in0=oS[:, h * 512:(h + 1) * 512], in1=exb[:, :], op=ALU.add)
            wk.inc(wk.V and ins, "s_ve")

    OFFP = [0, 128, 384, 768]

    def emit_segment(wk, s):
        barrier(wk)
        wk.w(wk.S, "s_xb")
        ins = wk.S and wk.S.dma_start(out=xT.rearrange("p (k f) -> p k f", f=128),
                                      in_=xbf.rearrange("p (k f) -> p k f", f=128), transpose=True)
        wk.inc(ins, "s_tr", 16)
        tr_xT = wk.c["s_tr"]
        if s + 1 < NSEG:
            wk.w(wk.G, "s_tr", tr_xT)
            ins = wk.G and wk.G.dma_start(
                out=xbf.rearrange("p (tc c) -> p tc c", c=D),
                in_=x_d[(s + 1) * SEG:(s + 2) * SEG, :].rearrange("(tc p) c -> p tc c", p=128))
            wk.inc(ins, "s_xb", 16)
        wk.w(wk.T, "s_tr", tr_xT)
        wk.w(wk.T, "s_ld", 16 * 8)
        proj_qk(wk, "q")
        proj_qk(wk, "k")
        for tc in range(4):
            for half in range(2):
                bi = 2 + (tc * 2 + half) % 2
                bank = ps[bi]
                wk.bank_wait(bi)
                if wk.T:
                    for j in range(NC8):
                        wk.T.matmul(bank[:, 0:512],
                                    xT[:, (tc * 8 + j) * 128:(tc * 8 + j) * 128 + 128],
                                    wv_sb[:, j * D + half * 512: j * D + half * 512 + 512],
                                    start=(j == 0), stop=False)
                    ins = wk.T.matmul(bank[:, 0:512], onesrow[0:1, :],
                                      bvrow[0:1, half * 512:(half + 1) * 512], start=False, stop=True)
                wk.inc(wk.T and ins, "s_mm")
                wk.w(wk.A, "s_mm")
                ins = wk.A and wk.A.activation(out=vbf[:, tc * D + half * 512: tc * D + (half + 1) * 512],
                                               in_=bank[:, 0:512], func=AF.Copy)
                wk.inc(ins, "s_ev")
                wk.bank_set(bi, "s_ev")
        wk.w(wk.S, "s_ve")
        ins = wk.S and wk.S.dma_start(out=sk_tok.rearrange("p (k f) -> p k f", f=128),
                                      in_=skT.rearrange("p (k f) -> p k f", f=128), transpose=True)
        wk.inc(ins, "s_tr", 16)
        if DEBUG2 and s == 0:
            for dst, src in ((dbg_xT, xT), (dbg_qT, qT), (dbg_kT, kT), (dbg_sq, sqT), (dbg_v, vbf)):
                dbg_dump(wk, dst[:, :], src[:, :])
        tr_head = {}
        stf_free = {0: 0, 1: 0}
        proj_done_ev = wk.c["s_ev"]
        proj_done_ve = wk.c["s_ve"]
        tr_sk = wk.c["s_tr"]

        def sm_pre(h, i4):
            par = h % 2
            base = 64 * par
            bidx = (4 - 4 * par) + i4 % 2
            bank = ps[bidx]
            psb = P_sb if par == 0 else P_sb2
            N = (i4 + 1) * 128
            if i4 == 0:
                wk.w(wk.T, "s_ev", proj_done_ev)
            wk.bank_wait(bidx)
            ins = wk.T and wk.T.matmul(bank[:, 0:N],
                                       qT[:, h * 512 + i4 * 128: h * 512 + (i4 + 1) * 128],
                                       kT[:, h * 512: h * 512 + N], start=True, stop=True)
            wk.inc(wk.T and ins, "s_mm")
            wk.w(wk.V, "s_mm")
            wk.w(wk.V, "s_ev", stf_free[par])
            ins = wk.V and wk.V.tensor_reduce(out=stf[:, base:base + 1], in_=bank[:, 0:N], axis=AX.X,
                                              op=ALU.max, negate=True)
            vf(wk, wk.V and ins)
            ins = wk.V and wk.V.tensor_scalar(out=stf[:, base + 2:base + 3], in0=stf[:, base:base + 1],
                                              scalar1=ISD, scalar2=None, op0=ALU.mult)
            wk.inc(wk.V and ins, "s_ve")
            if i4 == 0:
                wk.w(wk.A, "s_tr", tr_head.get(h - 2, 0))
            wk.w(wk.A, "s_ve")
            if wk.A:
                if i4 > 0:
                    wk.A.activation(out=psb[:, OFFP[i4]: OFFP[i4] + i4 * 128],
                                    in_=bank[:, 0:i4 * 128], func=AF.Exp, scale=ISD,
                                    bias=stf[:, base + 2:base + 3], accum_out=stf[:, base + 4:base + 5])
                ins = wk.A.activation(out=psb[:, OFFP[i4] + i4 * 128: OFFP[i4] + N],
                                      in_=bank[:, i4 * 128:N], func=AF.Exp, scale=ISD,
                                      bias=stf[:, base + 2:base + 3])
            wk.inc(wk.A and ins, "s_ev")
            wk.bank_set(bidx, "s_ev")
            stf_free[par] = wk.c["s_ev"]

        def sm_post(h, i4):
            par = h % 2
            base = 64 * par
            psb = P_sb if par == 0 else P_sb2
            N = (i4 + 1) * 128
            wk.w(wk.V, "s_ev", stf_free[par])
            dg = psb[:, OFFP[i4] + i4 * 128: OFFP[i4] + N]
            if wk.V:
                wk.V.tensor_tensor(out=dg, in0=dg, in1=tril[:, :], op=ALU.mult)
            ins = wk.V and wk.V.tensor_reduce(out=stf[:, base + 6:base + 7], in_=dg, axis=AX.X, op=ALU.add)
            vf(wk, wk.V and ins)
            if i4 > 0:
                ins = wk.V and wk.V.tensor_tensor(out=stf[:, base + 8:base + 9], in0=stf[:, base + 4:base + 5],
                                                  in1=stf[:, base + 6:base + 7], op=ALU.add)
                vf(wk, wk.V and ins)
                rs_col = stf[:, base + 8:base + 9]
            else:
                rs_col = stf[:, base + 6:base + 7]
            ins = wk.V and wk.V.reciprocal(out=stf[:, base + 10:base + 11], in_=rs_col)
            vf(wk, wk.V and ins)
            ins = wk.V and wk.V.tensor_tensor(out=stf[:, base + 32 + 2 * i4:base + 33 + 2 * i4],
                                              in0=stf[:, base + 10:base + 11],
                                              in1=gm8[:, h:h + 1], op=ALU.mult)
            wk.inc(wk.V and ins, "s_ve")

        def transpose_P(h):
            psb = P_sb if h % 2 == 0 else P_sb2
            wk.w(wk.S, "s_ve")
            wk.w(wk.S, "s_mm")
            ins = wk.S and wk.S.dma_start(out=PT_sb.rearrange("p (k f) -> p k f", f=128),
                                          in_=psb.rearrange("p (k f) -> p k f", f=128), transpose=True)
            wk.inc(ins, "s_tr", 16)
            tr_head[h] = wk.c["s_tr"]

        def mem_round(h, i4):
            par = h % 2
            base = 64 * par
            midx = 6 - 4 * par
            if i4 == 0:
                wk.w(wk.T, "s_ve", proj_done_ve)
            wk.bank_wait(midx)
            if wk.T:
                wk.T.matmul(ps[midx][:, 0:129],
                            sqT[:, h * 512 + i4 * 128: h * 512 + (i4 + 1) * 128],
                            Mb[:, h * 132: h * 132 + 129], start=True, stop=True)
                ins = wk.T.matmul(ps[midx][:, 136:265],
                                  skT[:, h * 512 + i4 * 128: h * 512 + (i4 + 1) * 128],
                                  Mb[:, h * 132: h * 132 + 129], start=True, stop=True)
            wk.inc(wk.T and ins, "s_mm")
            wk.w(wk.V, "s_mm")
            if wk.V:
                wk.V.tensor_single_scalar(out=stf[:, base + 16:base + 17], in_=ps[midx][:, 128:129],
                                          scalar=EPS, op=ALU.add)
                ins = wk.V.tensor_single_scalar(out=stf[:, base + 22:base + 23], in_=ps[midx][:, 264:265],
                                                scalar=EPS, op=ALU.add)
            else:
                ins = None
            vf(wk, wk.V and ins)
            if wk.V:
                wk.V.reciprocal(out=stf[:, base + 18:base + 19], in_=stf[:, base + 16:base + 17])
                ins = wk.V.reciprocal(out=stf[:, base + 24:base + 25], in_=stf[:, base + 22:base + 23])
            vf(wk, wk.V and ins)
            ins = wk.V and wk.V.tensor_tensor(out=stf[:, base + 20:base + 21], in0=stf[:, base + 18:base + 19],
                                              in1=g8[:, h:h + 1], op=ALU.mult)
            vf(wk, wk.V and ins)
            if wk.V:
                wk.V.tensor_scalar(out=attn[:, i4 * D + h * 128: i4 * D + (h + 1) * 128],
                                   in0=ps[midx][:, 0:128], scalar1=stf[:, base + 20:base + 21],
                                   scalar2=None, op0=ALU.mult)
                ins = wk.V.tensor_scalar(out=rk_tmp[:, :], in0=ps[midx][:, 136:264],
                                         scalar1=stf[:, base + 24:base + 25], scalar2=None, op0=ALU.mult)
            vf(wk, wk.V and ins)
            ins = wk.V and wk.V.tensor_tensor(out=u_sb[:, i4 * 136: i4 * 136 + 128],
                                              in0=vbf[:, i4 * D + h * 128: i4 * D + (h + 1) * 128],
                                              in1=rk_tmp[:, :], op=ALU.subtract)
            wk.inc(wk.V and ins, "s_ve")
            wk.bank_set(midx, "s_ve")

        def m_update(h):
            par = h % 2
            midx = 6 - 4 * par
            wk.bank_wait(midx)
            wk.w(wk.T, "s_ve")
            wk.w(wk.T, "s_tr", tr_sk)
            if wk.T:
                for tc in range(4):
                    ins = wk.T.matmul(ps[midx][:, 0:129],
                                      sk_tok[:, (h * 4 + tc) * 128:(h * 4 + tc) * 128 + 128],
                                      u_sb[:, tc * 136: tc * 136 + 129],
                                      start=(tc == 0), stop=(tc == 3))
            wk.inc(wk.T and ins, "s_mm")
            wk.w(wk.V, "s_mm")
            ins = wk.V and wk.V.tensor_tensor(out=Mf[:, h * 132: h * 132 + 129],
                                              in0=Mf[:, h * 132: h * 132 + 129],
                                              in1=ps[midx][:, 0:129], op=ALU.add)
            vf(wk, wk.V and ins)
            wk.bank_set(midx, "s_vf")
            ins = wk.V and wk.V.tensor_copy(out=Mb[:, h * 132: h * 132 + 129], in_=Mf[:, h * 132: h * 132 + 129])
            wk.inc(wk.V and ins, "s_ve")

        def a_dot(h):
            par = h % 2
            base = 64 * par
            aidx = 7 - 4 * par
            for i4 in range(4):
                wk.w(wk.T, "s_tr", tr_head[h])
                wk.bank_wait(aidx)
                if wk.T:
                    for j in range(i4 + 1):
                        ins = wk.T.matmul(ps[aidx][:, 0:128],
                                          PT_sb[:, OFFP[i4] + j * 128: OFFP[i4] + (j + 1) * 128],
                                          vbf[:, j * D + h * 128: j * D + (h + 1) * 128],
                                          start=(j == 0), stop=(j == i4))
                wk.inc(wk.T and ins, "s_mm")
                wk.w(wk.V, "s_mm")
                ins = wk.V and wk.V.scalar_tensor_tensor(
                    out=attn[:, i4 * D + h * 128: i4 * D + (h + 1) * 128],
                    in0=ps[aidx][:, 0:128], scalar=stf[:, base + 32 + 2 * i4:base + 33 + 2 * i4],
                    in1=attn[:, i4 * D + h * 128: i4 * D + (h + 1) * 128],
                    op0=ALU.mult, op1=ALU.add)
                wk.inc(wk.V and ins, "s_ve")
                wk.bank_set(aidx, "s_ve")

        for i4 in range(4):
            sm_pre(0, i4)
            sm_post(0, i4)
        transpose_P(0)
        for h in range(NH):
            for i4 in range(4):
                if h + 1 < NH:
                    sm_pre(h + 1, i4)
                mem_round(h, i4)
                if h + 1 < NH:
                    sm_post(h + 1, i4)
            m_update(h)
            a_dot(h)
            if h + 1 < NH:
                transpose_P(h + 1)
            tgt = min((s * NH + h + 1) * len(WCH) // (NSEG * NH), len(WCH))
            while wk.wq < tgt:
                w_emit_chunk(wk, wk.wq)
                wk.wq += 1
        for tc in range(4):
            wk.w(wk.V, "s_x")
            wk.w(wk.V, "s_ve")
            ins = wk.V and wk.V.tensor_tensor(out=attn[:, tc * D:(tc + 1) * D],
                                              in0=attn[:, tc * D:(tc + 1) * D],
                                              in1=xf[:, (tc % 2) * D:(tc % 2 + 1) * D], op=ALU.add)
            wk.inc(wk.V and ins, "s_ve")
            wk.w(wk.A, "s_y")
            ln_apply(wk, attn[:, tc * D:(tc + 1) * D], ybuf[:, :], ybuf[:, :])
            ins = wk.A and wk.A.dma_start(out=y_d[s * SEG + tc * 128: s * SEG + (tc + 1) * 128, :],
                                          in_=ybuf[:, :])
            wk.inc(ins, "s_y", 16)
            nseg, ntc = (s, tc + 2) if tc < 2 else (s + 1, tc - 2)
            if nseg < NSEG:
                ins = wk.A and wk.A.dma_start(out=xf[:, (tc % 2) * D:(tc % 2 + 1) * D],
                                              in_=x_d[nseg * SEG + ntc * 128: nseg * SEG + (ntc + 1) * 128, :])
                wk.inc(ins, "s_x", 16)

    def emit_phase_b(wk):
        pipe = QPipe(wk)
        barrier(wk)
        wk.w(wk.A, "s_ws")
        wk.w(wk.A, "s_y")
        ins = wk.A and wk.A.dma_start(out=wdq_sb[:, :], in_=wdq_d[:, :])
        wk.inc(ins, "s_pn", 16)

        def load_panels(nb):
            ndr = 4 if nb < 5 else 2
            w0 = nb * 32 * 128
            w1 = (nb * 32 + ndr * 8) * 128
            ins = wk.A and wk.A.dma_start(out=pnl_g[:, 0:w1 - w0], in_=wgq_d[:, w0:w1])
            wk.inc(ins, "s_pn", 16)
            ins = wk.A and wk.A.dma_start(out=pnl_u[:, 0:w1 - w0], in_=wuq_d[:, w0:w1])
            wk.inc(ins, "s_pn", 16)

        for t8 in range(NSEG):
            barrier(wk)
            for tc in range(4):
                ins = wk.A and wk.A.dma_start(out=y4[:, tc * D:(tc + 1) * D],
                                              in_=y_d[t8 * SEG + tc * 128: t8 * SEG + (tc + 1) * 128, :])
                wk.inc(ins, "s_yl", 16)
            wk.w(wk.V, "s_yl")
            for tc in range(4):
                for sl in range(2):
                    pipe.push(y4[:, tc * D + sl * 512: tc * D + (sl + 1) * 512],
                              xq[:, tc * D + sl * 512: tc * D + (sl + 1) * 512], 512)
            pipe.flush()
            wk.w(wk.S, "s_gp")
            ins = wk.S and wk.S.dma_start(out=xqT.rearrange("p (k f) -> p k f", f=128),
                                          in_=xq.rearrange("p (k f) -> p k f", f=128), transpose=True)
            wk.inc(ins, "s_tr", 16)
            tr_xq = wk.c["s_tr"]
            load_panels(0)
            for nb in range(6):
                wid = NBW[nb]
                ndr = 4 if nb < 5 else 2
                pn_now = wk.c["s_pn"]
                for tc in range(4):
                    bg = ps[tc % 2]
                    bu = ps[2 + tc % 2]
                    wk.w(wk.T, "s_tr", tr_xq)
                    wk.w(wk.T, "s_pn", pn_now)
                    wk.bank_wait(tc % 2)
                    wk.bank_wait(2 + tc % 2)
                    if wk.T:
                        gv = pnl_g[:, 0:ndr * 8 * 128].rearrange("p (dr j f) -> p j dr f", j=8, f=128)
                        uv = pnl_u[:, 0:ndr * 8 * 128].rearrange("p (dr j f) -> p j dr f", j=8, f=128)
                        for j in range(NC8):
                            wk.T.matmul(bg[:, 0:wid],
                                        xqT[:, (tc * 8 + j) * 128:(tc * 8 + j) * 128 + 128],
                                        gv[:, j, :, :], start=(j == 0), stop=(j == NC8 - 1))
                        for j in range(NC8):
                            ins = wk.T.matmul(bu[:, 0:wid],
                                              xqT[:, (tc * 8 + j) * 128:(tc * 8 + j) * 128 + 128],
                                              uv[:, j, :, :], start=(j == 0), stop=(j == NC8 - 1))
                    wk.inc(wk.T and ins, "s_mm")
                    wk.w(wk.A, "s_mm")
                    wk.w(wk.A, "s_ve", wk.silb_free)
                    if wk.A:
                        wk.A.activation(out=silb[:, 0:wid], in_=bg[:, 0:wid], func=AF.Exp, scale=-1.0)
                        wk.A.activation(out=silb[:, 0:wid], in_=silb[:, 0:wid], func=AF.Ln, bias=1.0)
                        ins = wk.A.activation(out=silb[:, 0:wid], in_=silb[:, 0:wid], func=AF.Exp, scale=-1.0)
                    wk.inc(wk.A and ins, "s_ev")
                    wk.w(wk.V, "s_ev")
                    par = tc % 2
                    ins = wk.V and wk.V.tensor_tensor(out=silb[:, 0:wid], in0=silb[:, 0:wid],
                                                      in1=bg[:, 0:wid], op=ALU.mult)
                    vf(wk, wk.V and ins)
                    ins = wk.V and wk.V.tensor_tensor(out=h_sl[:, par * 512: par * 512 + wid],
                                                      in0=silb[:, 0:wid], in1=bu[:, 0:wid], op=ALU.mult)
                    wk.inc(wk.V and ins, "s_ve")
                    wk.bank_set(tc % 2, "s_ve")
                    wk.bank_set(2 + tc % 2, "s_ve")
                    wk.silb_free = wk.c["s_ve"]
                    pipe.push(h_sl[:, par * 512: par * 512 + wid],
                              hq_full[:, tc * II + NBO[nb]: tc * II + NBO[nb] + wid], wid)
                if nb + 1 < 6:
                    wk.w(wk.A, "s_mm")
                    load_panels(nb + 1)
            pipe.flush()
            for tc in range(4):
                par = tc % 2
                wk.w(wk.S, "s_gp")
                wk.w(wk.S, "s_mm")
                ins = wk.S and wk.S.dma_start(
                    out=hqT[:, par * II: par * II + II].rearrange("p (k f) -> p k f", f=128),
                    in_=hq_full[:, tc * II:(tc + 1) * II].rearrange("p (k f) -> p k f", f=128),
                    transpose=True)
                wk.inc(ins, "s_tr", 16)
                for half in range(2):
                    bank = ps[4 + half]
                    wk.w(wk.T, "s_tr")
                    wk.w(wk.T, "s_pn")
                    wk.bank_wait(4 + half)
                    if wk.T:
                        wv_ = wdq_sb.rearrange("p (rn i f) -> p i rn f", rn=8, f=128)
                        for i22 in range(NI):
                            ins = wk.T.matmul(bank[:, 0:512],
                                              hqT[:, par * II + i22 * 128: par * II + (i22 + 1) * 128],
                                              wv_[:, i22, half * 4:(half + 1) * 4, :],
                                              start=(i22 == 0), stop=(i22 == NI - 1))
                    wk.inc(wk.T and ins, "s_mm")
                    wk.w(wk.V, "s_mm")
                    wk.w(wk.V, "s_ev")
                    ins = wk.V and wk.V.tensor_tensor(out=r2b[:, half * 512:(half + 1) * 512],
                                                      in0=bank[:, 0:512],
                                                      in1=y4[:, tc * D + half * 512: tc * D + (half + 1) * 512],
                                                      op=ALU.add)
                    wk.inc(wk.V and ins, "s_ve")
                    wk.bank_set(4 + half, "s_ve")
                wk.w(wk.A, "s_o")
                ln_apply(wk, r2b[:, :], obuf[:, :], obuf[:, :])
                ins = wk.A and wk.A.dma_start(out=out_d[t8 * SEG + tc * 128: t8 * SEG + (tc + 1) * 128, :],
                                              in_=obuf[:, :])
                wk.inc(ins, "s_o", 16)

    def emit_all(wk):
        emit_startup(wk)
        for s in range(NSEG):
            emit_segment(wk, s)
        emit_phase_b(wk)
        nst = 32 + (11 if DEBUG2 else 0)
        if DEBUG_Y:
            wk.w(wk.G, "s_y", 16 * 32)
            ins = wk.G and wk.G.dma_start(out=ydbg_d[:, :], in_=y_d[:, :])
            wk.inc(ins, "s_o", 16)
            nst += 1
        wk.w(wk.S, "s_o", 16 * nst)
        wk.w(wk.V, "s_o", 16 * nst)

    with nc.Block() as block:
        @block.sync
        def _(h):
            emit_all(Wk("sync", h))

        @block.scalar
        def _(h):
            emit_all(Wk("scalar", h))

        @block.vector
        def _(h):
            emit_all(Wk("vector", h))

        @block.tensor
        def _(h):
            emit_all(Wk("tensor", h))

        @block.gpsimd
        def _(h):
            emit_all(Wk("gpsimd", h))

    return nc


_NC_CACHE = []


def kernel(**inputs):
    f = lambda k: np.ascontiguousarray(np.asarray(inputs[k], dtype=np.float32))
    x = f("L0_attn_x")
    wq = f("L0_attn_w_q"); wkk = f("L0_attn_w_k"); wv = f("L0_attn_w_v")
    bq = f("L0_attn_b_q"); bk = f("L0_attn_b_k"); bv = f("L0_attn_b_v")
    beta = f("L0_attn_beta")
    wg = f("L0_ffn_w_gate"); wu = f("L0_ffn_w_up"); wd = f("L0_ffn_w_down")

    bq8 = np.ascontiguousarray(bq.reshape(8, 128).T)
    bk8 = np.ascontiguousarray(bk.reshape(8, 128).T)
    bvr = np.ascontiguousarray(bv.reshape(1, D))
    betar = np.ascontiguousarray(beta.reshape(1, NH))
    trilc = np.ascontiguousarray(np.tril(np.ones((128, 128), np.float32)).astype(ml_dtypes.bfloat16))

    if not _NC_CACHE:
        _NC_CACHE.append(build_program())
    nc = _NC_CACHE[0]
    common = dict(wq=wq, wk=wkk, wv=wv, bq8=bq8, bk8=bk8, bvrow=bvr, beta=betar,
                  wg=wg, wu=wu, wd=wd, tril=trilc)
    in_maps = [dict(common, x=np.ascontiguousarray(x[b])) for b in range(8)]
    res = run_bass_kernel_spmd(nc, in_maps, core_ids=list(range(8)))
    out = np.stack([np.asarray(r["out"]) for r in res.results], axis=0)
    return out.astype(np.float32)
